# version 19
# baseline (speedup 1.0000x reference)
import contextlib
import math
import numpy as np

import concourse.bass as bass
import concourse.mybir as mybir
from concourse.bass_utils import run_bass_kernel_spmd

F32 = mybir.dt.float32
BF16 = mybir.dt.bfloat16
AF = mybir.ActivationFunctionType
ALU = mybir.AluOpType
AX = mybir.AxisListType.X

T = 2048
D = 1024
H = 16
E = 64
FF = 4096
NT = 16
KD = 8

SC = 1.0 / math.sqrt(32.0)
BI = 4.0 / math.sqrt(32.0)
KB = 1024


def strip_chunks(m):
    t0 = 128 * m
    first_end = 512 * (m // 4) + 512
    out = [(t0, first_end - t0)]
    b = first_end
    while b < T:
        out.append((b, 512))
        b += 512
    return out


def build_program():
    nc = bass.Bass("TRN2", target_bir_lowering=False)

    x_d = nc.dram_tensor("x", [T, D], F32, kind="ExternalInput").ap()
    wqT_d = nc.dram_tensor("wqT", [D, 256], F32, kind="ExternalInput").ap()
    wkT_d = nc.dram_tensor("wkT", [D, 256], F32, kind="ExternalInput").ap()
    wvT_d = nc.dram_tensor("wvT", [D, D], F32, kind="ExternalInput").ap()
    woT_d = nc.dram_tensor("woT", [D, D], F32, kind="ExternalInput").ap()
    g_d = nc.dram_tensor("g128", [128, 1], F32, kind="ExternalInput").ap()
    w1T_d = nc.dram_tensor("w1T", [D, FF], F32, kind="ExternalInput").ap()
    b1_d = nc.dram_tensor("b1r", [128, 32], F32, kind="ExternalInput").ap()
    w2T_d = nc.dram_tensor("w2T", [FF, D], F32, kind="ExternalInput").ap()
    b2_d = nc.dram_tensor("b2r", [1, D], F32, kind="ExternalInput").ap()
    out_d = nc.dram_tensor("out", [T, D], F32, kind="ExternalOutput").ap()
    w2t_scr = nc.dram_tensor("w2t_scr", [32, 128, D], BF16, kind="Internal").ap()
    hg_scr = nc.dram_tensor("hg_scr", [32, 128, T], BF16, kind="Internal").ap()

    SBUF_BASE = 16512

    def at(name, shape, dt, off_kb):
        return nc.alloc_sbuf_tensor_at(name, shape, dt,
                                       offset=SBUF_BASE + int(off_kb * KB))

    xnT = at("xnT", [128, NT, KD, 128], BF16, 0)
    w12_sb = at("w12_sb", [128, 32768], BF16, 32)
    jw1a = at("jw1a", [128, 2, T], BF16, 144)
    jw1b = at("jw1b", [128, 2, T], BF16, 152)
    jw2a = at("jw2a", [128, 2, D], BF16, 96)
    jw2b = at("jw2b", [128, 2, D], BF16, 100)
    p1st = at("p1st", [128, 2, T], F32, 32)
    x_roll = at("x_roll", [128, 6, D], F32, 64)
    ajunk = at("ajunk", [128, T], BF16, 88)
    oneh = at("oneh", [128, 128], BF16, 200.25)
    tril05 = at("tril05", [128, 128], BF16, 200.5)
    wv_bf = at("wv_bf", [128, KD, D], BF16, 48)
    yT = at("yT", [128, KD, T], BF16, 64)
    xn_tm = at("xn_tm", [128, NT, D], BF16, 96)
    y_tm = at("y_tm", [128, 4, NT, 128], BF16, 96)
    wo_bf = at("wo_bf", [128, KD, D], BF16, 96)
    b2_bf = at("b2_bf", [1, D], BF16, 112)
    hgst = at("hgst", [128, 2, 512], BF16, 114)
    v_tm = at("v_tm", [128, NT, D], BF16, 128)
    wst = at("wst", [128, 2, T], F32, 128)
    hg_ts = at("hg_ts", [128, 2, 32, 256], BF16, 128)
    pv = at("pv", [128, 2, 512], BF16, 188)
    w2tb = at("w2tb", [128, 2, D], BF16, 152)
    wq_bf = at("wq_bf", [128, KD, 256], BF16, 168)
    qT = at("qT", [128, 2, T], BF16, 160)
    junk = at("junk", [128, 2, D], BF16, 160)
    ysq = at("ysq", [128, D], BF16, 190)
    kT = at("kT", [128, 2, T], BF16, 168)
    x_re = at("x_re", [128, 2, D], F32, 168)
    u_sb = at("u_sb", [128, 2, D], F32, 176)
    zlhs = at("zlhs", [128, 128], BF16, 200)
    A_buf = at("A_buf", [128, 4, T], BF16, 112)
    wk_bf = at("wk_bf", [128, KD, 256], BF16, 188)
    qh_ext = at("qh_ext", [128, 2, T], BF16, 192)
    out_sb = at("out_sb", [128, 2, D], F32, 192)
    consts = at("consts", [128, 384], F32, 204)
    tril_bf = at("tril_bf", [128, 128], BF16, 205.5)
    ones1_bf = at("ones1_bf", [1, 128], BF16, 205.75)
    zrow_bf = at("zrow_bf", [1, 128], BF16, 207)
    rinv = at("rinv", [128, NT, H], F32, 206)


    C_S1, C_SS, C_A, C_SD, C_RC = 0, 16, 32, 48, 64
    C_W1P, C_W2P = 80, 96
    C_W1S, C_W2S, C_ONE, C_G = 112, 113, 114, 115
    C_T1, C_T1N, C_T2, C_T2N = 116, 117, 118, 119
    C_B1 = 128
    C_TMP = 160
    C_TMPB = 192
    C_THA, C_THB = 176, 177
    C_CBI, C_CSC, C_CS20, C_CI64, C_CEPS = 184, 185, 186, 187, 188
    C_CZ, C_CO = 189, 190
    C_TOT = 180
    C_OR = 256

    def cc(col, n=1):
        return consts[:, col:col + n]

    nc.const_aps.register(F32, 0.0, cc(C_CZ))
    nc.const_aps.register(F32, 1.0, cc(C_CO))

    psY = nc.alloc_psum_tensor("psY", [128, T], F32)
    psS = [nc.alloc_psum_tensor(f"psS{i}", [128, 512], F32) for i in range(4)]

    ctx = contextlib.ExitStack()
    def S(name):
        return ctx.enter_context(nc.semaphore(name))

    class DS:
        def __init__(self, name, n=2):
            self.n = n
            self.s = [S(name + chr(97 + k)) for k in range(n)]

        def inc(self, dma, i):
            dma.then_inc(self.s[i % self.n], 16)

        def wait_done(self, e, i):
            e.wait_ge(self.s[i % self.n], 16 * (i // self.n + 1))

        def wait_all(self, e, cnt):
            for k in range(self.n):
                if (cnt - k + self.n - 1) // self.n > 0:
                    e.wait_ge(self.s[k], 16 * ((cnt - k + self.n - 1) // self.n))
    dX = DS("dX", 6); sW = S("sW"); sWv = S("sWv"); sWb1 = S("sWb1"); sG0 = S("sG0")
    sLNu = S("sLNu"); sLNa2 = S("sLNa2"); sLNsq = S("sLNsq"); sLNsd = S("sLNsd")
    sXN = S("sXN"); dXNT = DS("dXNT")
    sPQ = S("sPQ"); sPE = S("sPE"); sVM = S("sVM")
    sEXT = S("sEXT"); sSC = S("sSC"); sPOLY = S("sPOLY"); sPOLYV = S("sPOLYV"); sDM = S("sDM")
    sAV = S("sAV"); sYB = S("sYB"); sYE = S("sYE"); sYT = S("sYT")
    sQZ = S("sQZ")
    dP1L = DS("dP1L"); sP1R = S("sP1R"); sTHR = S("sTHR"); sZL = S("sZL")
    sRMa = S("sRMa"); sRMb = S("sRMb"); sRMc = S("sRMc"); sRMcS = S("sRMcS")
    sLNr = S("sLNr"); sLNrc = S("sLNrc"); sRMr = S("sRMr")
    sFV = S("sFV")
    sYTT = S("sYTT"); dYTT = DS("dYTT"); sWOG = S("sWOG"); sWOGs = S("sWOGs")
    sWO = S("sWO"); sWOE = S("sWOE"); dOUT1 = DS("dOUT1"); dXRE = DS("dXRE")
    dW1L = DS("dW1L"); sW1T = S("sW1T")
    dW2L = DS("dW2L"); sW2T = S("sW2T"); dW2S = DS("dW2S"); dW2F = DS("dW2F")
    sH = S("sH"); sHGs = S("sHGs"); dHGW = DS("dHGW"); dHGR = DS("dHGR")
    sO2 = S("sO2"); sO2E = S("sO2E"); dOUT2 = DS("dOUT2")

    poly_idx = {}
    dm_of = {}
    strip_of = {}
    n = 0; nd = 0; ns = 0
    for p in range(8):
        for m in range(NT):
            for hi in range(2):
                strip_of[(p, m, hi)] = ns; ns += 1
                for ci in range(len(strip_chunks(m))):
                    poly_idx[(p, m, hi, ci)] = n; n += 1
                dm_of[(p, m, hi)] = nd; nd += 1
    N_CH = n
    last_sc_of_pair = {p: poly_idx[(p, NT - 1, 1, 0)] + 1 for p in range(8)}
    owner = {}
    cntS = {}
    cntV = {}
    nS = 0
    nV = 0
    for idx in range(N_CH):
        if idx % 3 == 2:
            owner[idx] = "V"
            nV += 1
        else:
            owner[idx] = "S"
            nS += 1
        cntS[idx] = nS
        cntV[idx] = nV
    N_CH_S, N_CH_V = nS, nV

    def xt(t):
        return x_roll[:, t % 6, :]

    def blk(ap2d, nb):
        return ap2d.rearrange("p (b i) -> p b i", b=nb, i=128)

    def sync_stream(e):
        e.dma_start(out=cc(C_G), in_=g_d).then_inc(sG0, 16)
        e.dma_start(out=consts[:, C_B1:C_B1 + 32], in_=b1_d).then_inc(sWb1, 16)
        for t in range(NT):
            if t >= 6:
                e.wait_ge(sLNu, t - 5)
            dX.inc(e.dma_start(out=xt(t), in_=x_d[t * 128:(t + 1) * 128, :]), t)
        for t in range(NT):
            e.wait_ge(sXN, t + 1)
            if t >= 2:
                dXNT.wait_done(e, t - 2)
            dXNT.inc(e.dma_start(out=xnT[:, t, :, :],
                                 in_=blk(xn_tm[:, t, :], KD),
                                 transpose=True), t)
        e.wait_ge(sPE, 8)
        for h in (0, 1):
            r0 = 16 * (h % 8)
            e.wait_ge(sQZ, 1)
            e.dma_start(out=qh_ext[r0:r0 + 16, h % 2, :],
                        in_=qT[r0:r0 + 16, h // 8, :]).then_inc(sEXT, 16)
        for p in range(8):
            for h in (2 * p + 2, 2 * p + 3):
                if h < 16:
                    e.wait_ge(sSC, last_sc_of_pair[p])
                    e.wait_ge(sQZ, 2 * p - 1 if p >= 2 else 1)
                    r0 = 16 * (h % 8)
                    e.dma_start(out=qh_ext[r0:r0 + 16, h % 2, :],
                                in_=qT[r0:r0 + 16, h // 8, :]).then_inc(sEXT, 16)
            if p >= 3:
                pp = p - 3
                e.wait_ge(sRMc, NT * (pp + 1))
                e.wait_ge(sRMcS, NT * (pp + 1))
                if pp >= 2:
                    dYTT.wait_done(e, pp - 2)
                dYTT.inc(e.dma_start(out=blk(yT[:, pp, :], NT),
                                     in_=y_tm[:, pp % 4, :, :],
                                     transpose=True), pp)
        for pp in (5, 6, 7):
            e.wait_ge(sRMc, NT * (pp + 1))
            e.wait_ge(sRMcS, NT * (pp + 1))
            dYTT.wait_done(e, pp - 2)
            dYTT.inc(e.dma_start(out=blk(yT[:, pp, :], NT),
                                 in_=y_tm[:, pp % 4, :, :],
                                 transpose=True), pp)
        for t in range(NT):
            e.wait_ge(sWOE, 2 * (t + 1))
            dOUT1.inc(e.dma_start(out=out_d[t * 128:(t + 1) * 128, :],
                                  in_=out_sb[:, t % 2, :]), t)
        for i in range(128):
            tb, j = i // 32, i % 32
            e.wait_ge(sHGs, i + 1)
            dHGW.inc(e.dma_start(out=hg_scr[j][:, tb * 512:(tb + 1) * 512],
                                 in_=hgst[:, i % 2, :]), i)
        e.wait_ge(sH, 128)
        dW2S.wait_all(e, 32)
        for j in range(32):
            if j >= 2:
                dW2F.wait_done(e, j - 2)
            dW2F.inc(e.dma_start(out=w12_sb[:, j * D:(j + 1) * D],
                                 in_=w2t_scr[j]), j)
        for k in range(8):
            dHGW.wait_all(e, 32 * (k // 2 + 1))
            if k >= 1:
                e.wait_ge(sO2, 4 * k - 2)
            for part in range(4):
                j0 = part * 8
                src = bass.AP(hg_scr.tensor, (j0 * 128) * T + 2 * k * 128,
                              [[T, 128], [128 * T, 8], [1, 256]])
                d = e.dma_start(out=hg_ts[:, k % 2, j0:j0 + 8, :], in_=src)
                d.then_inc(dHGR.s[k % 2], 16)

    def gp_stream(e):
        e.dma_start(out=wq_bf[:], in_=wqT_d.rearrange("(c p) f -> p c f", p=128)
                    ).then_inc(sW, 16)
        e.dma_start(out=wk_bf[:], in_=wkT_d.rearrange("(c p) f -> p c f", p=128)
                    ).then_inc(sW, 16)
        e.iota(u_sb[:, 0, 0:128], [[1, 128]], base=0, channel_multiplier=-1,
               allow_small_or_imprecise_dtypes=True).then_inc(sG0, 1)
        e.dma_start(out=wv_bf[:], in_=wvT_d.rearrange("(c p) f -> p c f", p=128)
                    ).then_inc(sWv, 16)
        w1v = w1T_d.rearrange("(c p) f -> p c f", p=128)
        w2v = w2T_d.rearrange("(c p) f -> p c f", p=128)
        for i in range(32):
            if i >= 2:
                e.wait_ge(sP1R, i - 1)
            if i < 16:
                src = w1v[:, i // 2, (i % 2) * T:(i % 2 + 1) * T]
            else:
                src = w2v[:, 2 * (i - 16):2 * (i - 16) + 2, :]
            dst = p1st[:, i % 2, :]
            if i >= 16:
                dst = dst.rearrange("p (a b) -> p a b", a=2, b=1024)
            dP1L.inc(e.dma_start(out=dst, in_=src), i)
        e.wait_ge(sXN, NT)
        e.memset(zlhs[:], 0.0).then_inc(sZL, 1)
        dYTT.wait_all(e, KD)
        e.dma_start(out=wo_bf[:], in_=woT_d.rearrange("(c p) f -> p c f", p=128)
                    ).then_inc(sWOG, 16)
        e.dma_start(out=b2_bf[:], in_=b2_d).then_inc(sWOG, 16)
        e.wait_ge(sYE, 128)
        for u in range(16):
            c, hf = u // 2, u % 2
            if u >= 2:
                e.wait_ge(sW1T, u - 1)
            dW1L.inc(e.dma_start(out=wst[:, u % 2, :],
                                 in_=w1T_d[c * 128:(c + 1) * 128,
                                           hf * T:(hf + 1) * T]), u)
        e.wait_ge(sW1T, 16)
        for j in range(32):
            if j >= 2:
                e.wait_ge(sW2T, j - 1)
            dW2L.inc(e.dma_start(out=wst[:, j % 2, 0:D],
                                 in_=w2T_d[j * 128:(j + 1) * 128, :]), j)
            e.wait_ge(sW2T, j + 1)
            dW2S.inc(e.dma_start(out=w2t_scr[j], in_=w2tb[:, j % 2, :]), j)
        dOUT1.wait_all(e, NT)
        for t in (0, 1):
            dXRE.inc(e.dma_start(out=x_re[:, t, :],
                                 in_=x_d[t * 128:(t + 1) * 128, :]), t)
        for ts in range(NT):
            if ts + 2 < NT:
                tt = ts + 2
                e.wait_ge(sO2E, 2 * (tt - 1))
                dXRE.inc(e.dma_start(out=x_re[:, tt % 2, :],
                                     in_=x_d[tt * 128:(tt + 1) * 128, :]), tt)
            e.wait_ge(sO2E, 2 * (ts + 1))
            dOUT2.inc(e.dma_start(out=out_d[ts * 128:(ts + 1) * 128, :],
                                  in_=out_sb[:, ts % 2, :],
                                  accum_op=ALU.add), ts)
        dOUT1.wait_all(e, NT)
        dOUT2.wait_all(e, NT)

    def vec_stream(e):
        e.wait_ge(sG0, 17)
        fv = [0]

        def pub(ins):
            fv[0] += 1
            ins.then_inc(sFV, 1)

        def fence():
            e.wait_ge(sFV, fv[0])

        pub(e.tensor_single_scalar(tril_bf[:], u_sb[:, 0, 0:128], 0.0,
                                   ALU.is_ge))
        e.memset(qh_ext[:], 0.0).then_inc(sQZ, 1)
        pub(e.memset(ones1_bf[:], 1.0))
        pub(e.memset(zrow_bf[:], 0.0))
        pub(e.memset(oneh[:], 0.5))
        pub(e.memset(cc(C_ONE), 1.0))
        pub(e.memset(consts[0:1, C_OR:C_OR + 128], 1.0))
        pub(e.memset(cc(C_CBI), BI))
        pub(e.memset(cc(C_CSC), SC))
        pub(e.memset(cc(C_CS20), float(2.0 ** 20)))
        pub(e.memset(cc(C_CI64), 1.0 / E))
        pub(e.memset(cc(C_CEPS), 1e-5))
        pub(e.memset(cc(C_CZ), 0.0))
        pub(e.memset(cc(C_CO), 1.0))
        fence()
        e.tensor_scalar(tril05[:], tril_bf[:], 0.5, None,
                        ALU.mult).then_inc(sVM, 1)
        fence()
        for tp in range(0, NT, 2):
            for t in (tp, tp + 1):
                dX.wait_done(e, t)
                e.tensor_reduce(cc(C_S1 + t), xt(t), AX,
                                ALU.add).then_inc(sLNr, 1)
            for t in (tp, tp + 1):
                if t >= 2:
                    e.wait_ge(sLNsq, t - 1)
                e.wait_ge(sLNr, t + 1)
                e.tensor_scalar(u_sb[:, t % 2, :], xt(t), 1024.0,
                                cc(C_S1 + t), ALU.mult,
                                ALU.subtract).then_inc(sLNu, 1)
            for t in (tp, tp + 1):
                e.wait_ge(sLNsq, t + 1)
                e.tensor_scalar(cc(C_A + t), cc(C_SS + t), 2.0 ** -30, 1e-5,
                                ALU.mult, ALU.add).then_inc(sLNa2, 1)
            for t in (tp, tp + 1):
                e.wait_ge(sLNsd, t + 1)
                e.reciprocal(cc(C_RC + t), cc(C_SD + t)).then_inc(sLNrc, 1)
            for t in (tp, tp + 1):
                e.wait_ge(sLNrc, t + 1)
                e.tensor_scalar(xn_tm[:, t, :], u_sb[:, t % 2, :],
                                cc(C_RC + t), None, ALU.mult).then_inc(sXN, 1)
        pvk = [0]

        def rms_vec(p):
            e.wait_ge(sYE, 16 * (p + 1))
            for k in range(2):
                if p + k >= 1:
                    e.wait_ge(sRMa, 2 * p + k)
                yv = y_tm[:, p % 4, 8 * k:8 * k + 8, :]
                pub(e.tensor_tensor(blk(ysq[:], 8), yv, yv, ALU.mult))
                fence()
                rv = rinv[:, 8 * k:8 * k + 8, 2 * p:2 * p + 2]
                e.tensor_reduce(rv,
                                ysq[:].rearrange("p (t b w) -> p t b w",
                                                 t=8, b=2, w=E),
                                AX, ALU.add).then_inc(sRMa, 1)

        def rms_vec2(p):
            e.wait_ge(sRMb, p + 1)
            e.reciprocal(rinv[:, :, 2 * p:2 * p + 2],
                         rinv[:, :, 2 * p:2 * p + 2]).then_inc(sRMr, 1)
            e.wait_ge(sRMr, p + 1)
            h = 2 * p
            for t in range(NT):
                sl = y_tm[:, p % 4, t, 0:E]
                e.tensor_scalar(sl, sl, rinv[:, t, h:h + 1], None,
                                ALU.mult).then_inc(sRMc, 1)

        def evac_t(p, t):
            if t == 0 and p >= 4:
                dYTT.wait_done(e, p - 4)
            e.wait_ge(sYB, 4 * p + t // 4 + 1)
            e.tensor_copy(y_tm[:, p % 4, t, :],
                          psY[:, 128 * t:128 * (t + 1)]).then_inc(sYE, 1)

        for p in range(8):
            if p >= 1:
                b0 = 32 * ((p - 1) % 4)
                for hs in (0, 1):
                    e.wait_ge(sSC, last_sc_of_pair[p - 1])
                    e.memset(qh_ext[b0:b0 + 32, hs, :], 0.0).then_inc(sQZ, 1)
                rms_vec(p - 1)
            if p >= 2:
                rms_vec2(p - 2)
            for m in range(NT):
                chs = strip_chunks(m)
                for hi in range(2):
                    gs = strip_of[(p, m, hi)]
                    gated = False
                    for ci, (t0, w) in enumerate(chs):
                        idx = poly_idx[(p, m, hi, ci)]
                        if owner[idx] != "V":
                            continue
                        if gs >= 4 and not gated:
                            e.wait_ge(sAV, gs - 3)
                            gated = True
                        e.wait_ge(sSC, idx + 1)
                        slot = pvk[0] % 2
                        pvk[0] += 1
                        pub(e.tensor_scalar(pv[:, slot, 0:w],
                                            psS[idx % 4][:, 0:w],
                                            SC, BI, ALU.mult, ALU.add))
                        fence()
                        e.tensor_tensor(A_buf[:, gs % 4, t0:t0 + w],
                                        pv[:, slot, 0:w], pv[:, slot, 0:w],
                                        ALU.mult).then_inc(sPOLYV, 1)
                for hi in range(2):
                    gs = strip_of[(p, m, hi)]
                    idx = poly_idx[(p, m, hi, 0)]
                    if owner[idx] == "S":
                        e.wait_ge(sPOLY, cntS[idx])
                    else:
                        e.wait_ge(sPOLYV, cntV[idx])
                    sl = A_buf[:, gs % 4, 128 * m:128 * m + 128]
                    e.tensor_tensor(sl, sl, tril_bf[:],
                                    ALU.mult).then_inc(sDM, 1)
                if m >= 4 and m % 4 == 0:
                    for tt in range(m - 4, m):
                        evac_t(p, tt)
            for tt in range(12, 16):
                evac_t(p, tt)
        rms_vec(7)
        rms_vec2(6)
        rms_vec2(7)
        e.wait_ge(sP1R, 32)
        pub(e.tensor_reduce(cc(C_W1S), consts[:, C_W1P:C_W1P + 16], AX,
                            ALU.add))
        fence()
        e.tensor_reduce(cc(C_W2S), consts[:, C_W2P:C_W2P + 16], AX,
                        ALU.add).then_inc(sTHR, 1)
        e.wait_ge(sTHR, 4)
        e.tensor_scalar(cc(C_THA), psS[1][:, 0:1], 1.0 / (FF * D), 1e-8,
                        ALU.mult, ALU.max).then_inc(sTHR, 1)
        e.tensor_scalar(cc(C_THB), psS[1][:, 1:2], 1.0 / (FF * D), 1e-8,
                        ALU.mult, ALU.max).then_inc(sTHR, 1)
        e.wait_ge(sTHR, 6)
        e.tensor_scalar(cc(C_T1), cc(C_THA), 0.5, None,
                        ALU.mult).then_inc(sTHR, 1)
        e.tensor_scalar(cc(C_T1N), cc(C_THA), -0.5, None,
                        ALU.mult).then_inc(sTHR, 1)
        e.tensor_scalar(cc(C_T2), cc(C_THB), 0.5, None,
                        ALU.mult).then_inc(sTHR, 1)
        e.tensor_scalar(cc(C_T2N), cc(C_THB), -0.5, None,
                        ALU.mult).then_inc(sTHR, 1)
        e.wait_ge(sWOG, 32)
        for c in range(KD):
            e.tensor_scalar(wo_bf[:, c, :], wo_bf[:, c, :], cc(C_G),
                            None, ALU.mult).then_inc(sWOGs, 1)
        def w1_tern_unit(u):
            c, hf = u // 2, u % 2
            e.wait_ge(sTHR, 10)
            dW1L.wait_done(e, u)
            if c >= 4:
                e.wait_ge(sWO, 2 * NT)
            srcv = wst[:, u % 2, :]
            pub(e.tensor_single_scalar(jw1a[:, u % 2, :], srcv, cc(C_T1),
                                       ALU.is_gt))
            pub(e.tensor_single_scalar(jw1b[:, u % 2, :], srcv, cc(C_T1N),
                                       ALU.is_lt))
            fence()
            e.tensor_tensor(w12_sb[:, c * FF + hf * T:c * FF + (hf + 1) * T],
                            jw1a[:, u % 2, :], jw1b[:, u % 2, :],
                            ALU.subtract).then_inc(sW1T, 1)

        def wo_bank_v(i):
            if i % 4 < 2:
                return psS[2 + (i % 4)][:]
            b = i % 4 - 2
            return psY[:, 512 * b:512 * (b + 1)]

        for u in range(16):
            w1_tern_unit(u)
        def o2_bank_v(i):
            if i % 4 < 2:
                return psS[2 + (i % 4)][:]
            b = i % 4 - 2
            return psY[:, 512 * b:512 * (b + 1)]

        def o2_evac(ts):
            for dh in range(2):
                i = 2 * ts + dh
                ps = o2_bank_v(i)
                e.wait_ge(sO2, i + 1)
                dXRE.wait_done(e, ts)
                if ts >= 2:
                    dOUT2.wait_done(e, ts - 2)
                e.tensor_tensor(out_sb[:, ts % 2, dh * 512:(dh + 1) * 512],
                                ps, x_re[:, ts % 2, dh * 512:(dh + 1) * 512],
                                ALU.add).then_inc(sO2E, 1)

        for j in range(32):
            e.wait_ge(sTHR, 10)
            dW2L.wait_done(e, j)
            if j >= 2:
                dW2S.wait_done(e, j - 2)
            src = wst[:, j % 2, 0:D]
            pub(e.tensor_single_scalar(jw2a[:, j % 2, :], src, cc(C_T2),
                                       ALU.is_gt))
            pub(e.tensor_single_scalar(jw2b[:, j % 2, :], src, cc(C_T2N),
                                       ALU.is_lt))
            fence()
            e.tensor_tensor(w2tb[:, j % 2, :], jw2a[:, j % 2, :],
                            jw2b[:, j % 2, :],
                            ALU.subtract).then_inc(sW2T, 1)
        dOUT1.wait_all(e, NT)
        for ts in range(NT):
            o2_evac(ts)

    proj = ([("q", fc, ts) for fc in range(2) for ts in range(4)]
            + [("k", fc, ts) for fc in range(2) for ts in range(4)]
            + [("v", eh, tt) for eh in range(2) for tt in range(NT)])

    def sca_stream(e):
        ab_i = [0]

        def abs_unit():
            i = ab_i[0]
            if i >= 32:
                return
            ab_i[0] += 1
            if i >= 1:
                e.wait_ge(sP1R, i)
            dP1L.wait_done(e, i)
            col = (C_W1P + i) if i < 16 else (C_W2P + i - 16)
            e.activation(ajunk[:], p1st[:, i % 2, :], AF.Abs,
                         accum_out=cc(col)).then_inc(sP1R, 1)

        for tp in range(0, NT, 2):
            for t in (tp, tp + 1):
                e.wait_ge(sLNu, t + 1)
                e.activation(junk[:, t % 2, :], u_sb[:, t % 2, :], AF.Square,
                             accum_out=cc(C_SS + t)).then_inc(sLNsq, 1)
            abs_unit()
            abs_unit()
            for t in (tp, tp + 1):
                e.wait_ge(sLNa2, t + 1)
                e.activation(cc(C_SD + t), cc(C_A + t), AF.Sqrt,
                             scale=cc(C_CS20)).then_inc(sLNsd, 1)
            abs_unit()
            abs_unit()
        while ab_i[0] < 32:
            abs_unit()
        for i, (kind, a, b) in enumerate(proj):
            ps = psS[i % 4]
            e.wait_ge(sPQ, i + 1)
            if kind == "q":
                dst = qT[:, a, b * 512:(b + 1) * 512]
            elif kind == "k":
                dst = kT[:, a, b * 512:(b + 1) * 512]
            else:
                dst = v_tm[:, b, a * 512:(a + 1) * 512]
            e.copy(dst, ps[:]).then_inc(sPE, 1)
        def rms_sca(p):
            e.wait_ge(sRMa, 2 * (p + 1))
            e.activation(rinv[:, :, 2 * p:2 * p + 2],
                         rinv[:, :, 2 * p:2 * p + 2], AF.Sqrt,
                         scale=cc(C_CI64), bias=cc(C_CEPS)).then_inc(sRMb, 1)

        def rms_sca2(p):
            e.wait_ge(sRMr, p + 1)
            h = 2 * p + 1
            for t in range(NT):
                sl = y_tm[:, p % 4, t, E:2 * E]
                e.activation(sl, sl, AF.Copy,
                             scale=rinv[:, t, h:h + 1]).then_inc(sRMcS, 1)

        for p in range(8):
            if p >= 2:
                rms_sca(p - 2)
            if p >= 3:
                rms_sca2(p - 3)
            for m in range(NT):
                chs = strip_chunks(m)
                for hi in range(2):
                    gs = strip_of[(p, m, hi)]
                    gated = False
                    for ci, (t0, w) in enumerate(chs):
                        idx = poly_idx[(p, m, hi, ci)]
                        if owner[idx] != "S":
                            continue
                        if gs >= 4 and not gated:
                            e.wait_ge(sAV, gs - 3)
                            gated = True
                        e.wait_ge(sSC, idx + 1)
                        e.activation(A_buf[:, gs % 4, t0:t0 + w],
                                     psS[idx % 4][:, 0:w], AF.Square,
                                     bias=cc(C_CBI),
                                     scale=cc(C_CSC)).then_inc(sPOLY, 1)
        for p in (6, 7):
            rms_sca(p)
        for p in (5, 6, 7):
            rms_sca2(p)
        e.wait_ge(sTHR, 2)
        e.copy(consts[0:1, C_TOT:C_TOT + 2],
               psS[0][0:1, 0:2]).then_inc(sTHR, 1)
        def wo_bank_s(i):
            if i % 4 < 2:
                return psS[2 + (i % 4)][:]
            b = i % 4 - 2
            return psY[:, 512 * b:512 * (b + 1)]

        for t in range(NT):
            for dh in range(2):
                i = 2 * t + dh
                ps = wo_bank_s(i)
                e.wait_ge(sWO, i + 1)
                if t >= 2:
                    dOUT1.wait_done(e, t - 2)
                e.copy(out_sb[:, t % 2, dh * 512:(dh + 1) * 512],
                       ps).then_inc(sWOE, 1)
        e.wait_ge(sWb1, 16)
        for i in range(128):
            j = i % 32
            ps = psS[i % 2]
            e.wait_ge(sH, i + 1)
            if i >= 2:
                dHGW.wait_done(e, i - 2)
            e.activation(hgst[:, i % 2, :], ps[:], AF.Gelu,
                         bias=consts[:, C_B1 + j:C_B1 + j + 1]).then_inc(sHGs, 1)
        def o2_bank_s(i):
            if i % 4 < 2:
                return psS[2 + (i % 4)][:]
            b = i % 4 - 2
            return psY[:, 512 * b:512 * (b + 1)]

        dOUT1.wait_all(e, NT)


    def ten_stream(e):
        e.wait_ge(sW, 32)
        for i, (kind, a, b) in enumerate(proj):
            ps = psS[i % 4]
            if i == 16:
                e.wait_ge(sWv, 16)
            if i >= 4:
                e.wait_ge(sPE, i - 3)
            if kind == "v":
                dXNT.wait_done(e, b)
            else:
                dXNT.wait_all(e, 4 * (b + 1))
            for c in range(KD):
                if kind == "q":
                    lhsT = wq_bf[:, c, a * 128:(a + 1) * 128]
                    rhs = xnT[:, 4 * b:4 * b + 4, c, :]
                elif kind == "k":
                    lhsT = wk_bf[:, c, a * 128:(a + 1) * 128]
                    rhs = xnT[:, 4 * b:4 * b + 4, c, :]
                else:
                    lhsT = xnT[:, b, c, :]
                    rhs = wv_bf[:, c, a * 512:(a + 1) * 512]
                mm = e.matmul(ps[:], lhsT, rhs, start=(c == 0),
                              stop=(c == KD - 1))
            mm.then_inc(sPQ, 1)
        e.wait_ge(sPE, 48)
        e.wait_ge(sVM, 1)

        def emit_av(e, p, m):
            chs = strip_chunks(m)

            def tile_ci(t):
                col = 128 * t
                for ci, (t0, w) in enumerate(chs):
                    if t0 <= col < t0 + w:
                        return ci
                raise AssertionError

            for hi, h in ((0, 2 * p), (1, 2 * p + 1)):
                gs = strip_of[(p, m, hi)]
                prev_ci = -1
                for t in list(range(m + 1, NT)) + [m]:
                    if t == m:
                        e.wait_ge(sDM, dm_of[(p, m, hi)] + 1)
                    else:
                        ci = tile_ci(t)
                        if ci != prev_ci:
                            idx = poly_idx[(p, m, hi, ci)]
                            if owner[idx] == "S":
                                e.wait_ge(sPOLY, cntS[idx])
                            else:
                                e.wait_ge(sPOLYV, cntV[idx])
                            prev_ci = ci
                    e.matmul(psY[:, 128 * t + 64 * hi:128 * t + 64 * hi + 64],
                             A_buf[:, gs % 4, 128 * t:128 * (t + 1)],
                             v_tm[:, m, h * E:(h + 1) * E],
                             start=False, stop=False)
                e.sem_inc(sAV, 1)

        def close_b(e, b):
            e.matmul(psY[:, 512 * b:512 * (b + 1)], zrow_bf[:],
                     v_tm[0:1, 0, 0:512], start=False,
                     stop=True).then_inc(sYB, 1)

        for p in range(8):
            for b in range(4):
                if p > 0:
                    e.wait_ge(sYE, 16 * (p - 1) + 4 * (b + 1))
                e.matmul(psY[:, 512 * b:512 * (b + 1)], zrow_bf[:],
                         v_tm[0:1, 0, 0:512], start=True, stop=False)
            for mm_ in range(NT):
                for t in range(mm_, NT):
                    e.matmul(psY[:, 128 * t:128 * (t + 1)],
                             tril05[:] if t == mm_ else oneh[:],
                             v_tm[:, mm_, 128 * p:128 * (p + 1)],
                             start=False, stop=False)
            for m in range(NT):
                chs = strip_chunks(m)
                if m == 0 and p >= 1:
                    e.wait_ge(sQZ, 1 + 2 * p)
                for hi, h in ((0, 2 * p), (1, 2 * p + 1)):
                    if m == 0 and hi == 0:
                        e.wait_ge(sEXT, 32 * (p + 1))
                    for ci, (t0, w) in enumerate(chs):
                        idx = poly_idx[(p, m, hi, ci)]
                        if idx >= 4:
                            oidx = idx - 4
                            if owner[oidx] == "S":
                                e.wait_ge(sPOLY, cntS[oidx])
                            else:
                                e.wait_ge(sPOLYV, cntV[oidx])
                        e.matmul(psS[idx % 4][:, 0:w],
                                 kT[:, h // 8, m * 128:(m + 1) * 128],
                                 qh_ext[:, hi, t0:t0 + w],
                                 start=True, stop=True).then_inc(sSC, 1)
                if m > 0:
                    emit_av(e, p, m - 1)
                    if (m - 1) % 4 == 3:
                        close_b(e, (m - 1) // 4)
            emit_av(e, p, NT - 1)
            close_b(e, 3)
        e.wait_ge(sPOLY, N_CH_S)
        e.wait_ge(sPOLYV, N_CH_V)
        e.wait_ge(sTHR, 1)
        e.matmul(psS[0][0:1, 0:1], cc(C_W1S), cc(C_ONE), start=True, stop=True)
        e.matmul(psS[0][0:1, 1:2], cc(C_W2S), cc(C_ONE), start=True,
                 stop=True).then_inc(sTHR, 1)
        e.wait_ge(sTHR, 3)
        e.matmul(psS[1][:, 0:1], consts[0:1, C_OR:C_OR + 128],
                 consts[0:1, C_TOT:C_TOT + 1], start=True, stop=True)
        e.matmul(psS[1][:, 1:2], consts[0:1, C_OR:C_OR + 128],
                 consts[0:1, C_TOT + 1:C_TOT + 2], start=True,
                 stop=True).then_inc(sTHR, 1)
        dYTT.wait_all(e, KD)
        e.wait_ge(sWOG, 32)
        e.wait_ge(sWOGs, 8)
        def wo_bank(i):
            if i % 4 < 2:
                return psS[2 + (i % 4)][:]
            b = i % 4 - 2
            return psY[:, 512 * b:512 * (b + 1)]

        for t in range(NT):
            for dh in range(2):
                i = 2 * t + dh
                ps = wo_bank(i)
                if i >= 4:
                    e.wait_ge(sWOE, i - 3)
                for c in range(KD):
                    e.matmul(ps, yT[:, c, t * 128:(t + 1) * 128],
                             wo_bf[:, c, dh * 512:(dh + 1) * 512],
                             start=(c == 0), stop=False)
                e.matmul(ps, ones1_bf[:],
                         b2_bf[0:1, dh * 512:(dh + 1) * 512],
                         start=False, stop=True).then_inc(sWO, 1)
        e.wait_ge(sW1T, 16)
        dXNT.wait_all(e, NT)
        for i in range(128):
            tb, j = i // 32, i % 32
            ps = psS[i % 2]
            if i >= 2:
                e.wait_ge(sHGs, i - 1)
            for c in range(KD):
                mm = e.matmul(ps[:],
                              w12_sb[:, c * FF + j * 128:c * FF + j * 128 + 128],
                              xnT[:, 4 * tb:4 * tb + 4, c, :],
                              start=(c == 0), stop=(c == KD - 1))
            mm.then_inc(sH, 1)
        def o2_bank(i):
            if i % 4 < 2:
                return psS[2 + (i % 4)][:]
            b = i % 4 - 2
            return psY[:, 512 * b:512 * (b + 1)]

        for ts in range(NT):
            k = ts // 2
            e.wait_ge(dHGR.s[k % 2], 64 * (k // 2 + 1))
            for dh in range(2):
                i = 2 * ts + dh
                ps = o2_bank(i)
                if i >= 4:
                    e.wait_ge(sO2E, i - 3)
                for j in range(32):
                    if ts == 0 and dh == 0:
                        dW2F.wait_done(e, j)
                    mm = e.matmul(ps,
                                  hg_ts[:, k % 2, j,
                                        (ts % 2) * 128:(ts % 2) * 128 + 128],
                                  w12_sb[:, j * D + dh * 512:j * D + (dh + 1) * 512],
                                  start=(j == 0), stop=(j == 31))
                mm.then_inc(sO2, 1)

    with ctx:
        with nc.Block() as block:
            block.sync(sync_stream)
            block.tensor(ten_stream)
            block.vector(vec_stream)
            block.scalar(sca_stream)
            block.gpsimd(gp_stream)
    return nc


def kernel(L0_attn_hidden_states, L0_attn_Wq, L0_attn_Wk, L0_attn_Wv,
           L0_attn_Wo, L0_attn_g_weight, L0_ffn_W1, L0_ffn_b1, L0_ffn_W2,
           L0_ffn_b2):
    x = np.ascontiguousarray(np.asarray(L0_attn_hidden_states, dtype=np.float32))
    wqT = np.ascontiguousarray(np.asarray(L0_attn_Wq, dtype=np.float32).T)
    wkT = np.ascontiguousarray(np.asarray(L0_attn_Wk, dtype=np.float32).T)
    wvT = np.ascontiguousarray(np.asarray(L0_attn_Wv, dtype=np.float32).T)
    woT = np.ascontiguousarray(np.asarray(L0_attn_Wo, dtype=np.float32).T)
    g128 = np.ascontiguousarray(np.tile(np.asarray(L0_attn_g_weight,
                                                   dtype=np.float32),
                                        2).reshape(128, 1))
    w1T = np.ascontiguousarray(np.asarray(L0_ffn_W1, dtype=np.float32).T)
    b1r = np.ascontiguousarray(np.asarray(L0_ffn_b1,
                                          dtype=np.float32).reshape(32, 128).T)
    w2T = np.ascontiguousarray(np.asarray(L0_ffn_W2, dtype=np.float32).T)
    b2r = np.ascontiguousarray(np.asarray(L0_ffn_b2,
                                          dtype=np.float32).reshape(1, D))

    nc = build_program()
    shared = {"wqT": wqT, "wkT": wkT, "wvT": wvT, "woT": woT, "g128": g128,
              "w1T": w1T, "b1r": b1r, "w2T": w2T, "b2r": b2r}
    in_maps = [dict(shared, x=np.ascontiguousarray(x[c])) for c in range(8)]
    res = run_bass_kernel_spmd(nc, in_maps, core_ids=list(range(8)))
    out = np.stack([np.asarray(r["out"], dtype=np.float32)
                    for r in res.results], axis=0)
    return out


# version 20
# speedup vs baseline: 1.0748x; 1.0748x over previous
import contextlib
import math
import numpy as np

import concourse.bass as bass
import concourse.mybir as mybir
from concourse.bass_utils import run_bass_kernel_spmd

F32 = mybir.dt.float32
BF16 = mybir.dt.bfloat16
AF = mybir.ActivationFunctionType
ALU = mybir.AluOpType
AX = mybir.AxisListType.X

T = 2048
D = 1024
H = 16
E = 64
FF = 4096
NT = 16
KD = 8

SC = 1.0 / math.sqrt(32.0)
BI = 4.0 / math.sqrt(32.0)
KB = 1024


def strip_chunks(m):
    t0 = 128 * m
    first_end = 512 * (m // 4) + 512
    out = [(t0, first_end - t0)]
    b = first_end
    while b < T:
        out.append((b, 512))
        b += 512
    return out


def build_program():
    nc = bass.Bass("TRN2", target_bir_lowering=False)

    x_d = nc.dram_tensor("x", [T, D], F32, kind="ExternalInput").ap()
    wqT_d = nc.dram_tensor("wqT", [D, 256], F32, kind="ExternalInput").ap()
    wkT_d = nc.dram_tensor("wkT", [D, 256], F32, kind="ExternalInput").ap()
    wvT_d = nc.dram_tensor("wvT", [D, D], F32, kind="ExternalInput").ap()
    woT_d = nc.dram_tensor("woT", [D, D], F32, kind="ExternalInput").ap()
    g_d = nc.dram_tensor("g128", [128, 1], F32, kind="ExternalInput").ap()
    w1T_d = nc.dram_tensor("w1T", [D, FF], F32, kind="ExternalInput").ap()
    b1_d = nc.dram_tensor("b1r", [128, 32], F32, kind="ExternalInput").ap()
    w2T_d = nc.dram_tensor("w2T", [FF, D], F32, kind="ExternalInput").ap()
    b2_d = nc.dram_tensor("b2r", [1, D], F32, kind="ExternalInput").ap()
    out_d = nc.dram_tensor("out", [T, D], F32, kind="ExternalOutput").ap()
    hg_scr = nc.dram_tensor("hg_scr", [32, 128, T], BF16, kind="Internal").ap()

    SBUF_BASE = 16512

    def at(name, shape, dt, off_kb):
        return nc.alloc_sbuf_tensor_at(name, shape, dt,
                                       offset=SBUF_BASE + int(off_kb * KB))

    xnT = at("xnT", [128, NT, KD, 128], BF16, 0)
    w12_sb = at("w12_sb", [128, 32768], BF16, 32)
    jw1a = at("jw1a", [128, 2, T], BF16, 144)
    jw1b = at("jw1b", [128, 2, T], BF16, 152)
    jw2a = at("jw2a", [128, 2, D], BF16, 160)
    jw2b = at("jw2b", [128, 2, D], BF16, 164)
    w2t_sb = at("w2t_sb", [128, 32, D], BF16, 96)
    p1st = at("p1st", [128, 2, T], F32, 32)
    x_roll = at("x_roll", [128, 6, D], F32, 64)
    ajunk = at("ajunk", [128, T], BF16, 88)
    oneh = at("oneh", [128, 128], BF16, 200.25)
    tril05 = at("tril05", [128, 128], BF16, 200.5)
    wv_bf = at("wv_bf", [128, KD, D], BF16, 48)
    yT = at("yT", [128, KD, T], BF16, 64)
    xn_tm = at("xn_tm", [128, NT, D], BF16, 96)
    y_tm = at("y_tm", [128, 4, NT, 128], BF16, 96)
    wo_bf = at("wo_bf", [128, KD, D], BF16, 96)
    b2_bf = at("b2_bf", [1, D], BF16, 201)
    hgst = at("hgst", [128, 2, 512], BF16, 201)
    v_tm = at("v_tm", [128, NT, D], BF16, 128)
    wst = at("wst", [128, 2, T], F32, 176)
    hg_ts = at("hg_ts", [128, 2, 32, 256], BF16, 32)
    pv = at("pv", [128, 2, 512], BF16, 188)
    wq_bf = at("wq_bf", [128, KD, 256], BF16, 168)
    qT = at("qT", [128, 2, T], BF16, 160)
    junk = at("junk", [128, 2, D], BF16, 160)
    ysq = at("ysq", [128, D], BF16, 190)
    kT = at("kT", [128, 2, T], BF16, 168)
    x_re = at("x_re", [128, 2, D], F32, 168)
    u_sb = at("u_sb", [128, 2, D], F32, 176)
    zlhs = at("zlhs", [128, 128], BF16, 200)
    A_buf = at("A_buf", [128, 4, T], BF16, 112)
    wk_bf = at("wk_bf", [128, KD, 256], BF16, 188)
    qh_ext = at("qh_ext", [128, 2, T], BF16, 192)
    out_sb = at("out_sb", [128, 2, D], F32, 192)
    consts = at("consts", [128, 384], F32, 204)
    tril_bf = at("tril_bf", [128, 128], BF16, 205.5)
    ones1_bf = at("ones1_bf", [1, 128], BF16, 205.75)
    zrow_bf = at("zrow_bf", [1, 128], BF16, 207)
    rinv = at("rinv", [128, NT, H], F32, 206)


    C_S1, C_SS, C_A, C_SD, C_RC = 0, 16, 32, 48, 64
    C_W1P, C_W2P = 80, 96
    C_W1S, C_W2S, C_ONE, C_G = 112, 113, 114, 115
    C_T1, C_T1N, C_T2, C_T2N = 116, 117, 118, 119
    C_B1 = 128
    C_TMP = 160
    C_TMPB = 192
    C_THA, C_THB = 176, 177
    C_CBI, C_CSC, C_CS20, C_CI64, C_CEPS = 184, 185, 186, 187, 188
    C_CZ, C_CO = 189, 190
    C_TOT = 180
    C_OR = 256

    def cc(col, n=1):
        return consts[:, col:col + n]

    nc.const_aps.register(F32, 0.0, cc(C_CZ))
    nc.const_aps.register(F32, 1.0, cc(C_CO))

    psY = nc.alloc_psum_tensor("psY", [128, T], F32)
    psS = [nc.alloc_psum_tensor(f"psS{i}", [128, 512], F32) for i in range(4)]

    ctx = contextlib.ExitStack()
    def S(name):
        return ctx.enter_context(nc.semaphore(name))

    class DS:
        def __init__(self, name, n=2):
            self.n = n
            self.s = [S(name + chr(97 + k)) for k in range(n)]

        def inc(self, dma, i):
            dma.then_inc(self.s[i % self.n], 16)

        def wait_done(self, e, i):
            e.wait_ge(self.s[i % self.n], 16 * (i // self.n + 1))

        def wait_all(self, e, cnt):
            for k in range(self.n):
                if (cnt - k + self.n - 1) // self.n > 0:
                    e.wait_ge(self.s[k], 16 * ((cnt - k + self.n - 1) // self.n))
    dX = DS("dX", 6); sW = S("sW"); sWv = S("sWv"); sWb1 = S("sWb1"); sG0 = S("sG0")
    sLNu = S("sLNu"); sLNa2 = S("sLNa2"); sLNsq = S("sLNsq"); sLNsd = S("sLNsd")
    sXN = S("sXN"); dXNT = DS("dXNT")
    sPQ = S("sPQ"); sPE = S("sPE"); sVM = S("sVM")
    sEXT = S("sEXT"); sSC = S("sSC"); sPOLY = S("sPOLY"); sPOLYV = S("sPOLYV"); sDM = S("sDM")
    sAV = S("sAV"); sYB = S("sYB"); sYE = S("sYE"); sYT = S("sYT")
    sQZ = S("sQZ")
    dP1L = DS("dP1L"); sP1R = S("sP1R"); sTHR = S("sTHR"); sZL = S("sZL")
    sRMa = S("sRMa"); sRMb = S("sRMb"); sRMc = S("sRMc"); sRMcS = S("sRMcS")
    sLNr = S("sLNr"); sLNrc = S("sLNrc"); sRMr = S("sRMr")
    sFV = S("sFV")
    sYTT = S("sYTT"); dYTT = DS("dYTT"); sWOG = S("sWOG"); sWOGs = S("sWOGs")
    sWO = S("sWO"); sWOE = S("sWOE"); dOUT1 = DS("dOUT1"); dXRE = DS("dXRE")
    dW1L = DS("dW1L"); sW1T = S("sW1T")
    dW2L = DS("dW2L"); sW2T = S("sW2T")
    sH = S("sH"); sHGs = S("sHGs"); dHGW = DS("dHGW"); dHGR = DS("dHGR")
    sO2 = S("sO2"); sO2E = S("sO2E"); dOUT2 = DS("dOUT2")

    poly_idx = {}
    dm_of = {}
    strip_of = {}
    n = 0; nd = 0; ns = 0
    for p in range(8):
        for m in range(NT):
            for hi in range(2):
                strip_of[(p, m, hi)] = ns; ns += 1
                for ci in range(len(strip_chunks(m))):
                    poly_idx[(p, m, hi, ci)] = n; n += 1
                dm_of[(p, m, hi)] = nd; nd += 1
    N_CH = n
    last_sc_of_pair = {p: poly_idx[(p, NT - 1, 1, 0)] + 1 for p in range(8)}
    owner = {}
    cntS = {}
    cntV = {}
    nS = 0
    nV = 0
    for idx in range(N_CH):
        if idx % 3 == 2:
            owner[idx] = "V"
            nV += 1
        else:
            owner[idx] = "S"
            nS += 1
        cntS[idx] = nS
        cntV[idx] = nV
    N_CH_S, N_CH_V = nS, nV

    def xt(t):
        return x_roll[:, t % 6, :]

    def blk(ap2d, nb):
        return ap2d.rearrange("p (b i) -> p b i", b=nb, i=128)

    def sync_stream(e):
        e.dma_start(out=cc(C_G), in_=g_d).then_inc(sG0, 16)
        e.dma_start(out=consts[:, C_B1:C_B1 + 32], in_=b1_d).then_inc(sWb1, 16)
        for t in range(NT):
            if t >= 6:
                e.wait_ge(sLNu, t - 5)
            dX.inc(e.dma_start(out=xt(t), in_=x_d[t * 128:(t + 1) * 128, :]), t)
        for t in range(NT):
            e.wait_ge(sXN, t + 1)
            if t >= 2:
                dXNT.wait_done(e, t - 2)
            dXNT.inc(e.dma_start(out=xnT[:, t, :, :],
                                 in_=blk(xn_tm[:, t, :], KD),
                                 transpose=True), t)
        e.wait_ge(sPE, 8)
        for h in (0, 1):
            r0 = 16 * (h % 8)
            e.wait_ge(sQZ, 1)
            e.dma_start(out=qh_ext[r0:r0 + 16, h % 2, :],
                        in_=qT[r0:r0 + 16, h // 8, :]).then_inc(sEXT, 16)
        for p in range(8):
            for h in (2 * p + 2, 2 * p + 3):
                if h < 16:
                    e.wait_ge(sSC, last_sc_of_pair[p])
                    e.wait_ge(sQZ, 2 * p - 1 if p >= 2 else 1)
                    r0 = 16 * (h % 8)
                    e.dma_start(out=qh_ext[r0:r0 + 16, h % 2, :],
                                in_=qT[r0:r0 + 16, h // 8, :]).then_inc(sEXT, 16)
            if p >= 3:
                pp = p - 3
                e.wait_ge(sRMc, NT * (pp + 1))
                e.wait_ge(sRMcS, NT * (pp + 1))
                if pp >= 2:
                    dYTT.wait_done(e, pp - 2)
                dYTT.inc(e.dma_start(out=blk(yT[:, pp, :], NT),
                                     in_=y_tm[:, pp % 4, :, :],
                                     transpose=True), pp)
        for pp in (5, 6, 7):
            e.wait_ge(sRMc, NT * (pp + 1))
            e.wait_ge(sRMcS, NT * (pp + 1))
            dYTT.wait_done(e, pp - 2)
            dYTT.inc(e.dma_start(out=blk(yT[:, pp, :], NT),
                                 in_=y_tm[:, pp % 4, :, :],
                                 transpose=True), pp)
        for t in range(NT):
            e.wait_ge(sWOE, 2 * (t + 1))
            dOUT1.inc(e.dma_start(out=out_d[t * 128:(t + 1) * 128, :],
                                  in_=out_sb[:, t % 2, :]), t)
        for i in range(128):
            tb, j = i // 32, i % 32
            e.wait_ge(sHGs, i + 1)
            dHGW.inc(e.dma_start(out=hg_scr[j][:, tb * 512:(tb + 1) * 512],
                                 in_=hgst[:, i % 2, :]), i)
        e.wait_ge(sH, 128)
        for k in range(8):
            dHGW.wait_all(e, 32 * (k // 2 + 1))
            if k >= 1:
                e.wait_ge(sO2, 4 * k - 2)
            for part in range(4):
                j0 = part * 8
                src = bass.AP(hg_scr.tensor, (j0 * 128) * T + 2 * k * 128,
                              [[T, 128], [128 * T, 8], [1, 256]])
                d = e.dma_start(out=hg_ts[:, k % 2, j0:j0 + 8, :], in_=src)
                d.then_inc(dHGR.s[k % 2], 16)

    def gp_stream(e):
        e.dma_start(out=wq_bf[:], in_=wqT_d.rearrange("(c p) f -> p c f", p=128)
                    ).then_inc(sW, 16)
        e.dma_start(out=wk_bf[:], in_=wkT_d.rearrange("(c p) f -> p c f", p=128)
                    ).then_inc(sW, 16)
        e.iota(u_sb[:, 0, 0:128], [[1, 128]], base=0, channel_multiplier=-1,
               allow_small_or_imprecise_dtypes=True).then_inc(sG0, 1)
        e.dma_start(out=wv_bf[:], in_=wvT_d.rearrange("(c p) f -> p c f", p=128)
                    ).then_inc(sWv, 16)
        w1v = w1T_d.rearrange("(c p) f -> p c f", p=128)
        w2v = w2T_d.rearrange("(c p) f -> p c f", p=128)
        for i in range(32):
            if i >= 2:
                e.wait_ge(sP1R, i - 1)
            if i < 16:
                src = w1v[:, i // 2, (i % 2) * T:(i % 2 + 1) * T]
            else:
                src = w2v[:, 2 * (i - 16):2 * (i - 16) + 2, :]
            dst = p1st[:, i % 2, :]
            if i >= 16:
                dst = dst.rearrange("p (a b) -> p a b", a=2, b=1024)
            dP1L.inc(e.dma_start(out=dst, in_=src), i)
        e.wait_ge(sXN, NT)
        e.memset(zlhs[:], 0.0).then_inc(sZL, 1)
        dYTT.wait_all(e, KD)
        e.dma_start(out=wo_bf[:], in_=woT_d.rearrange("(c p) f -> p c f", p=128)
                    ).then_inc(sWOG, 16)
        e.dma_start(out=b2_bf[:], in_=b2_d).then_inc(sWOG, 16)
        e.wait_ge(sYE, 128)
        for u in range(16):
            c, hf = u // 2, u % 2
            if u >= 2:
                e.wait_ge(sW1T, u - 1)
            dW1L.inc(e.dma_start(out=wst[:, u % 2, :],
                                 in_=w1T_d[c * 128:(c + 1) * 128,
                                           hf * T:(hf + 1) * T]), u)
        e.wait_ge(sW1T, 16)
        for j in range(32):
            if j >= 2:
                e.wait_ge(sW2T, j - 1)
            dW2L.inc(e.dma_start(out=wst[:, j % 2, 0:D],
                                 in_=w2T_d[j * 128:(j + 1) * 128, :]), j)
        dOUT1.wait_all(e, NT)
        for t in (0, 1):
            dXRE.inc(e.dma_start(out=x_re[:, t, :],
                                 in_=x_d[t * 128:(t + 1) * 128, :]), t)
        for ts in range(NT):
            if ts + 2 < NT:
                tt = ts + 2
                e.wait_ge(sO2E, 2 * (tt - 1))
                dXRE.inc(e.dma_start(out=x_re[:, tt % 2, :],
                                     in_=x_d[tt * 128:(tt + 1) * 128, :]), tt)
            e.wait_ge(sO2E, 2 * (ts + 1))
            dOUT2.inc(e.dma_start(out=out_d[ts * 128:(ts + 1) * 128, :],
                                  in_=out_sb[:, ts % 2, :],
                                  accum_op=ALU.add), ts)
        dOUT1.wait_all(e, NT)
        dOUT2.wait_all(e, NT)

    def vec_stream(e):
        e.wait_ge(sG0, 17)
        fv = [0]

        def pub(ins):
            fv[0] += 1
            ins.then_inc(sFV, 1)

        def fence():
            e.wait_ge(sFV, fv[0])

        pub(e.tensor_single_scalar(tril_bf[:], u_sb[:, 0, 0:128], 0.0,
                                   ALU.is_ge))
        e.memset(qh_ext[:], 0.0).then_inc(sQZ, 1)
        pub(e.memset(ones1_bf[:], 1.0))
        pub(e.memset(zrow_bf[:], 0.0))
        pub(e.memset(oneh[:], 0.5))
        pub(e.memset(cc(C_ONE), 1.0))
        pub(e.memset(consts[0:1, C_OR:C_OR + 128], 1.0))
        pub(e.memset(cc(C_CBI), BI))
        pub(e.memset(cc(C_CSC), SC))
        pub(e.memset(cc(C_CS20), float(2.0 ** 20)))
        pub(e.memset(cc(C_CI64), 1.0 / E))
        pub(e.memset(cc(C_CEPS), 1e-5))
        pub(e.memset(cc(C_CZ), 0.0))
        pub(e.memset(cc(C_CO), 1.0))
        fence()
        e.tensor_scalar(tril05[:], tril_bf[:], 0.5, None,
                        ALU.mult).then_inc(sVM, 1)
        fence()
        for tp in range(0, NT, 2):
            for t in (tp, tp + 1):
                dX.wait_done(e, t)
                e.tensor_reduce(cc(C_S1 + t), xt(t), AX,
                                ALU.add).then_inc(sLNr, 1)
            for t in (tp, tp + 1):
                if t >= 2:
                    e.wait_ge(sLNsq, t - 1)
                e.wait_ge(sLNr, t + 1)
                e.tensor_scalar(u_sb[:, t % 2, :], xt(t), 1024.0,
                                cc(C_S1 + t), ALU.mult,
                                ALU.subtract).then_inc(sLNu, 1)
            for t in (tp, tp + 1):
                e.wait_ge(sLNsq, t + 1)
                e.tensor_scalar(cc(C_A + t), cc(C_SS + t), 2.0 ** -30, 1e-5,
                                ALU.mult, ALU.add).then_inc(sLNa2, 1)
            for t in (tp, tp + 1):
                e.wait_ge(sLNsd, t + 1)
                e.reciprocal(cc(C_RC + t), cc(C_SD + t)).then_inc(sLNrc, 1)
            for t in (tp, tp + 1):
                e.wait_ge(sLNrc, t + 1)
                e.tensor_scalar(xn_tm[:, t, :], u_sb[:, t % 2, :],
                                cc(C_RC + t), None, ALU.mult).then_inc(sXN, 1)
        pvk = [0]

        def rms_vec(p):
            e.wait_ge(sYE, 16 * (p + 1))
            for k in range(2):
                if p + k >= 1:
                    e.wait_ge(sRMa, 2 * p + k)
                yv = y_tm[:, p % 4, 8 * k:8 * k + 8, :]
                pub(e.tensor_tensor(blk(ysq[:], 8), yv, yv, ALU.mult))
                fence()
                rv = rinv[:, 8 * k:8 * k + 8, 2 * p:2 * p + 2]
                e.tensor_reduce(rv,
                                ysq[:].rearrange("p (t b w) -> p t b w",
                                                 t=8, b=2, w=E),
                                AX, ALU.add).then_inc(sRMa, 1)

        def rms_vec2(p):
            e.wait_ge(sRMb, p + 1)
            e.reciprocal(rinv[:, :, 2 * p:2 * p + 2],
                         rinv[:, :, 2 * p:2 * p + 2]).then_inc(sRMr, 1)
            e.wait_ge(sRMr, p + 1)
            h = 2 * p
            for t in range(NT):
                sl = y_tm[:, p % 4, t, 0:E]
                e.tensor_scalar(sl, sl, rinv[:, t, h:h + 1], None,
                                ALU.mult).then_inc(sRMc, 1)

        def evac_t(p, t):
            if t == 0 and p >= 4:
                dYTT.wait_done(e, p - 4)
            e.wait_ge(sYB, 4 * p + t // 4 + 1)
            e.tensor_copy(y_tm[:, p % 4, t, :],
                          psY[:, 128 * t:128 * (t + 1)]).then_inc(sYE, 1)

        for p in range(8):
            if p >= 1:
                b0 = 32 * ((p - 1) % 4)
                for hs in (0, 1):
                    e.wait_ge(sSC, last_sc_of_pair[p - 1])
                    e.memset(qh_ext[b0:b0 + 32, hs, :], 0.0).then_inc(sQZ, 1)
                rms_vec(p - 1)
            if p >= 2:
                rms_vec2(p - 2)
            for m in range(NT):
                chs = strip_chunks(m)
                for hi in range(2):
                    gs = strip_of[(p, m, hi)]
                    gated = False
                    for ci, (t0, w) in enumerate(chs):
                        idx = poly_idx[(p, m, hi, ci)]
                        if owner[idx] != "V":
                            continue
                        if gs >= 4 and not gated:
                            e.wait_ge(sAV, gs - 3)
                            gated = True
                        e.wait_ge(sSC, idx + 1)
                        slot = pvk[0] % 2
                        pvk[0] += 1
                        pub(e.tensor_scalar(pv[:, slot, 0:w],
                                            psS[idx % 4][:, 0:w],
                                            SC, BI, ALU.mult, ALU.add))
                        fence()
                        e.tensor_tensor(A_buf[:, gs % 4, t0:t0 + w],
                                        pv[:, slot, 0:w], pv[:, slot, 0:w],
                                        ALU.mult).then_inc(sPOLYV, 1)
                for hi in range(2):
                    gs = strip_of[(p, m, hi)]
                    idx = poly_idx[(p, m, hi, 0)]
                    if owner[idx] == "S":
                        e.wait_ge(sPOLY, cntS[idx])
                    else:
                        e.wait_ge(sPOLYV, cntV[idx])
                    sl = A_buf[:, gs % 4, 128 * m:128 * m + 128]
                    e.tensor_tensor(sl, sl, tril_bf[:],
                                    ALU.mult).then_inc(sDM, 1)
                if m >= 4 and m % 4 == 0:
                    for tt in range(m - 4, m):
                        evac_t(p, tt)
            for tt in range(12, 16):
                evac_t(p, tt)
        rms_vec(7)
        rms_vec2(6)
        rms_vec2(7)
        e.wait_ge(sP1R, 32)
        pub(e.tensor_reduce(cc(C_W1S), consts[:, C_W1P:C_W1P + 16], AX,
                            ALU.add))
        fence()
        e.tensor_reduce(cc(C_W2S), consts[:, C_W2P:C_W2P + 16], AX,
                        ALU.add).then_inc(sTHR, 1)
        e.wait_ge(sTHR, 4)
        e.tensor_scalar(cc(C_THA), psS[1][:, 0:1], 1.0 / (FF * D), 1e-8,
                        ALU.mult, ALU.max).then_inc(sTHR, 1)
        e.tensor_scalar(cc(C_THB), psS[1][:, 1:2], 1.0 / (FF * D), 1e-8,
                        ALU.mult, ALU.max).then_inc(sTHR, 1)
        e.wait_ge(sTHR, 6)
        e.tensor_scalar(cc(C_T1), cc(C_THA), 0.5, None,
                        ALU.mult).then_inc(sTHR, 1)
        e.tensor_scalar(cc(C_T1N), cc(C_THA), -0.5, None,
                        ALU.mult).then_inc(sTHR, 1)
        e.tensor_scalar(cc(C_T2), cc(C_THB), 0.5, None,
                        ALU.mult).then_inc(sTHR, 1)
        e.tensor_scalar(cc(C_T2N), cc(C_THB), -0.5, None,
                        ALU.mult).then_inc(sTHR, 1)
        e.wait_ge(sWOG, 32)
        for c in range(KD):
            e.tensor_scalar(wo_bf[:, c, :], wo_bf[:, c, :], cc(C_G),
                            None, ALU.mult).then_inc(sWOGs, 1)
        def w1_tern_unit(u):
            c, hf = u // 2, u % 2
            e.wait_ge(sTHR, 10)
            dW1L.wait_done(e, u)
            if c >= 4:
                e.wait_ge(sWO, 2 * NT)
            srcv = wst[:, u % 2, :]
            pub(e.tensor_single_scalar(jw1a[:, u % 2, :], srcv, cc(C_T1),
                                       ALU.is_gt))
            pub(e.tensor_single_scalar(jw1b[:, u % 2, :], srcv, cc(C_T1N),
                                       ALU.is_lt))
            fence()
            e.tensor_tensor(w12_sb[:, c * FF + hf * T:c * FF + (hf + 1) * T],
                            jw1a[:, u % 2, :], jw1b[:, u % 2, :],
                            ALU.subtract).then_inc(sW1T, 1)

        def wo_bank_v(i):
            if i % 4 < 2:
                return psS[2 + (i % 4)][:]
            b = i % 4 - 2
            return psY[:, 512 * b:512 * (b + 1)]

        for u in range(16):
            w1_tern_unit(u)
        def o2_bank_v(i):
            if i % 4 < 2:
                return psS[2 + (i % 4)][:]
            b = i % 4 - 2
            return psY[:, 512 * b:512 * (b + 1)]

        def o2_evac(ts):
            for dh in range(2):
                i = 2 * ts + dh
                ps = o2_bank_v(i)
                e.wait_ge(sO2, i + 1)
                dXRE.wait_done(e, ts)
                if ts >= 2:
                    dOUT2.wait_done(e, ts - 2)
                e.tensor_tensor(out_sb[:, ts % 2, dh * 512:(dh + 1) * 512],
                                ps, x_re[:, ts % 2, dh * 512:(dh + 1) * 512],
                                ALU.add).then_inc(sO2E, 1)

        for j in range(32):
            e.wait_ge(sTHR, 10)
            if j == 0:
                e.wait_ge(sWO, 2 * NT)
            dW2L.wait_done(e, j)
            src = wst[:, j % 2, 0:D]
            pub(e.tensor_single_scalar(jw2a[:, j % 2, :], src, cc(C_T2),
                                       ALU.is_gt))
            pub(e.tensor_single_scalar(jw2b[:, j % 2, :], src, cc(C_T2N),
                                       ALU.is_lt))
            fence()
            e.tensor_tensor(w2t_sb[:, j, :], jw2a[:, j % 2, :],
                            jw2b[:, j % 2, :],
                            ALU.subtract).then_inc(sW2T, 1)
        dOUT1.wait_all(e, NT)
        for ts in range(NT):
            o2_evac(ts)

    proj = ([("q", fc, ts) for fc in range(2) for ts in range(4)]
            + [("k", fc, ts) for fc in range(2) for ts in range(4)]
            + [("v", eh, tt) for eh in range(2) for tt in range(NT)])

    def sca_stream(e):
        ab_i = [0]

        def abs_unit():
            i = ab_i[0]
            if i >= 32:
                return
            ab_i[0] += 1
            if i >= 1:
                e.wait_ge(sP1R, i)
            dP1L.wait_done(e, i)
            col = (C_W1P + i) if i < 16 else (C_W2P + i - 16)
            e.activation(ajunk[:], p1st[:, i % 2, :], AF.Abs,
                         accum_out=cc(col)).then_inc(sP1R, 1)

        for tp in range(0, NT, 2):
            for t in (tp, tp + 1):
                e.wait_ge(sLNu, t + 1)
                e.activation(junk[:, t % 2, :], u_sb[:, t % 2, :], AF.Square,
                             accum_out=cc(C_SS + t)).then_inc(sLNsq, 1)
            abs_unit()
            abs_unit()
            for t in (tp, tp + 1):
                e.wait_ge(sLNa2, t + 1)
                e.activation(cc(C_SD + t), cc(C_A + t), AF.Sqrt,
                             scale=cc(C_CS20)).then_inc(sLNsd, 1)
            abs_unit()
            abs_unit()
        while ab_i[0] < 32:
            abs_unit()
        for i, (kind, a, b) in enumerate(proj):
            ps = psS[i % 4]
            e.wait_ge(sPQ, i + 1)
            if kind == "q":
                dst = qT[:, a, b * 512:(b + 1) * 512]
            elif kind == "k":
                dst = kT[:, a, b * 512:(b + 1) * 512]
            else:
                dst = v_tm[:, b, a * 512:(a + 1) * 512]
            e.copy(dst, ps[:]).then_inc(sPE, 1)
        def rms_sca(p):
            e.wait_ge(sRMa, 2 * (p + 1))
            e.activation(rinv[:, :, 2 * p:2 * p + 2],
                         rinv[:, :, 2 * p:2 * p + 2], AF.Sqrt,
                         scale=cc(C_CI64), bias=cc(C_CEPS)).then_inc(sRMb, 1)

        def rms_sca2(p):
            e.wait_ge(sRMr, p + 1)
            h = 2 * p + 1
            for t in range(NT):
                sl = y_tm[:, p % 4, t, E:2 * E]
                e.activation(sl, sl, AF.Copy,
                             scale=rinv[:, t, h:h + 1]).then_inc(sRMcS, 1)

        for p in range(8):
            if p >= 2:
                rms_sca(p - 2)
            if p >= 3:
                rms_sca2(p - 3)
            for m in range(NT):
                chs = strip_chunks(m)
                for hi in range(2):
                    gs = strip_of[(p, m, hi)]
                    gated = False
                    for ci, (t0, w) in enumerate(chs):
                        idx = poly_idx[(p, m, hi, ci)]
                        if owner[idx] != "S":
                            continue
                        if gs >= 4 and not gated:
                            e.wait_ge(sAV, gs - 3)
                            gated = True
                        e.wait_ge(sSC, idx + 1)
                        e.activation(A_buf[:, gs % 4, t0:t0 + w],
                                     psS[idx % 4][:, 0:w], AF.Square,
                                     bias=cc(C_CBI),
                                     scale=cc(C_CSC)).then_inc(sPOLY, 1)
        for p in (6, 7):
            rms_sca(p)
        for p in (5, 6, 7):
            rms_sca2(p)
        e.wait_ge(sTHR, 2)
        e.copy(consts[0:1, C_TOT:C_TOT + 2],
               psS[0][0:1, 0:2]).then_inc(sTHR, 1)
        def wo_bank_s(i):
            if i % 4 < 2:
                return psS[2 + (i % 4)][:]
            b = i % 4 - 2
            return psY[:, 512 * b:512 * (b + 1)]

        for t in range(NT):
            for dh in range(2):
                i = 2 * t + dh
                ps = wo_bank_s(i)
                e.wait_ge(sWO, i + 1)
                if t >= 2:
                    dOUT1.wait_done(e, t - 2)
                e.copy(out_sb[:, t % 2, dh * 512:(dh + 1) * 512],
                       ps).then_inc(sWOE, 1)
        e.wait_ge(sWb1, 16)
        for i in range(128):
            j = i % 32
            ps = psS[i % 2]
            e.wait_ge(sH, i + 1)
            if i >= 2:
                dHGW.wait_done(e, i - 2)
            e.activation(hgst[:, i % 2, :], ps[:], AF.Gelu,
                         bias=consts[:, C_B1 + j:C_B1 + j + 1]).then_inc(sHGs, 1)
        def o2_bank_s(i):
            if i % 4 < 2:
                return psS[2 + (i % 4)][:]
            b = i % 4 - 2
            return psY[:, 512 * b:512 * (b + 1)]

        dOUT1.wait_all(e, NT)


    def ten_stream(e):
        e.wait_ge(sW, 32)
        for i, (kind, a, b) in enumerate(proj):
            ps = psS[i % 4]
            if i == 16:
                e.wait_ge(sWv, 16)
            if i >= 4:
                e.wait_ge(sPE, i - 3)
            if kind == "v":
                dXNT.wait_done(e, b)
            else:
                dXNT.wait_all(e, 4 * (b + 1))
            for c in range(KD):
                if kind == "q":
                    lhsT = wq_bf[:, c, a * 128:(a + 1) * 128]
                    rhs = xnT[:, 4 * b:4 * b + 4, c, :]
                elif kind == "k":
                    lhsT = wk_bf[:, c, a * 128:(a + 1) * 128]
                    rhs = xnT[:, 4 * b:4 * b + 4, c, :]
                else:
                    lhsT = xnT[:, b, c, :]
                    rhs = wv_bf[:, c, a * 512:(a + 1) * 512]
                mm = e.matmul(ps[:], lhsT, rhs, start=(c == 0),
                              stop=(c == KD - 1))
            mm.then_inc(sPQ, 1)
        e.wait_ge(sPE, 48)
        e.wait_ge(sVM, 1)

        def emit_av(e, p, m):
            chs = strip_chunks(m)

            def tile_ci(t):
                col = 128 * t
                for ci, (t0, w) in enumerate(chs):
                    if t0 <= col < t0 + w:
                        return ci
                raise AssertionError

            for hi, h in ((0, 2 * p), (1, 2 * p + 1)):
                gs = strip_of[(p, m, hi)]
                prev_ci = -1
                for t in list(range(m + 1, NT)) + [m]:
                    if t == m:
                        e.wait_ge(sDM, dm_of[(p, m, hi)] + 1)
                    else:
                        ci = tile_ci(t)
                        if ci != prev_ci:
                            idx = poly_idx[(p, m, hi, ci)]
                            if owner[idx] == "S":
                                e.wait_ge(sPOLY, cntS[idx])
                            else:
                                e.wait_ge(sPOLYV, cntV[idx])
                            prev_ci = ci
                    e.matmul(psY[:, 128 * t + 64 * hi:128 * t + 64 * hi + 64],
                             A_buf[:, gs % 4, 128 * t:128 * (t + 1)],
                             v_tm[:, m, h * E:(h + 1) * E],
                             start=False, stop=False)
                e.sem_inc(sAV, 1)

        def close_b(e, b):
            e.matmul(psY[:, 512 * b:512 * (b + 1)], zrow_bf[:],
                     v_tm[0:1, 0, 0:512], start=False,
                     stop=True).then_inc(sYB, 1)

        for p in range(8):
            for b in range(4):
                if p > 0:
                    e.wait_ge(sYE, 16 * (p - 1) + 4 * (b + 1))
                e.matmul(psY[:, 512 * b:512 * (b + 1)], zrow_bf[:],
                         v_tm[0:1, 0, 0:512], start=True, stop=False)
            for mm_ in range(NT):
                for t in range(mm_, NT):
                    e.matmul(psY[:, 128 * t:128 * (t + 1)],
                             tril05[:] if t == mm_ else oneh[:],
                             v_tm[:, mm_, 128 * p:128 * (p + 1)],
                             start=False, stop=False)
            for m in range(NT):
                chs = strip_chunks(m)
                if m == 0 and p >= 1:
                    e.wait_ge(sQZ, 1 + 2 * p)
                for hi, h in ((0, 2 * p), (1, 2 * p + 1)):
                    if m == 0 and hi == 0:
                        e.wait_ge(sEXT, 32 * (p + 1))
                    for ci, (t0, w) in enumerate(chs):
                        idx = poly_idx[(p, m, hi, ci)]
                        if idx >= 4:
                            oidx = idx - 4
                            if owner[oidx] == "S":
                                e.wait_ge(sPOLY, cntS[oidx])
                            else:
                                e.wait_ge(sPOLYV, cntV[oidx])
                        e.matmul(psS[idx % 4][:, 0:w],
                                 kT[:, h // 8, m * 128:(m + 1) * 128],
                                 qh_ext[:, hi, t0:t0 + w],
                                 start=True, stop=True).then_inc(sSC, 1)
                if m > 0:
                    emit_av(e, p, m - 1)
                    if (m - 1) % 4 == 3:
                        close_b(e, (m - 1) // 4)
            emit_av(e, p, NT - 1)
            close_b(e, 3)
        e.wait_ge(sPOLY, N_CH_S)
        e.wait_ge(sPOLYV, N_CH_V)
        e.wait_ge(sTHR, 1)
        e.matmul(psS[0][0:1, 0:1], cc(C_W1S), cc(C_ONE), start=True, stop=True)
        e.matmul(psS[0][0:1, 1:2], cc(C_W2S), cc(C_ONE), start=True,
                 stop=True).then_inc(sTHR, 1)
        e.wait_ge(sTHR, 3)
        e.matmul(psS[1][:, 0:1], consts[0:1, C_OR:C_OR + 128],
                 consts[0:1, C_TOT:C_TOT + 1], start=True, stop=True)
        e.matmul(psS[1][:, 1:2], consts[0:1, C_OR:C_OR + 128],
                 consts[0:1, C_TOT + 1:C_TOT + 2], start=True,
                 stop=True).then_inc(sTHR, 1)
        dYTT.wait_all(e, KD)
        e.wait_ge(sWOG, 32)
        e.wait_ge(sWOGs, 8)
        def wo_bank(i):
            if i % 4 < 2:
                return psS[2 + (i % 4)][:]
            b = i % 4 - 2
            return psY[:, 512 * b:512 * (b + 1)]

        for t in range(NT):
            for dh in range(2):
                i = 2 * t + dh
                ps = wo_bank(i)
                if i >= 4:
                    e.wait_ge(sWOE, i - 3)
                for c in range(KD):
                    e.matmul(ps, yT[:, c, t * 128:(t + 1) * 128],
                             wo_bf[:, c, dh * 512:(dh + 1) * 512],
                             start=(c == 0), stop=False)
                e.matmul(ps, ones1_bf[:],
                         b2_bf[0:1, dh * 512:(dh + 1) * 512],
                         start=False, stop=True).then_inc(sWO, 1)
        e.wait_ge(sW1T, 16)
        dXNT.wait_all(e, NT)
        for i in range(128):
            tb, j = i // 32, i % 32
            ps = psS[i % 2]
            if i >= 2:
                e.wait_ge(sHGs, i - 1)
            for c in range(KD):
                mm = e.matmul(ps[:],
                              w12_sb[:, c * FF + j * 128:c * FF + j * 128 + 128],
                              xnT[:, 4 * tb:4 * tb + 4, c, :],
                              start=(c == 0), stop=(c == KD - 1))
            mm.then_inc(sH, 1)
        def o2_bank(i):
            if i % 4 < 2:
                return psS[2 + (i % 4)][:]
            b = i % 4 - 2
            return psY[:, 512 * b:512 * (b + 1)]

        for ts in range(NT):
            k = ts // 2
            e.wait_ge(dHGR.s[k % 2], 64 * (k // 2 + 1))
            for dh in range(2):
                i = 2 * ts + dh
                ps = o2_bank(i)
                if i >= 4:
                    e.wait_ge(sO2E, i - 3)
                for j in range(32):
                    if ts == 0 and dh == 0:
                        e.wait_ge(sW2T, j + 1)
                    mm = e.matmul(ps,
                                  hg_ts[:, k % 2, j,
                                        (ts % 2) * 128:(ts % 2) * 128 + 128],
                                  w2t_sb[:, j, dh * 512:(dh + 1) * 512],
                                  start=(j == 0), stop=(j == 31))
                mm.then_inc(sO2, 1)

    with ctx:
        with nc.Block() as block:
            block.sync(sync_stream)
            block.tensor(ten_stream)
            block.vector(vec_stream)
            block.scalar(sca_stream)
            block.gpsimd(gp_stream)
    return nc


def kernel(L0_attn_hidden_states, L0_attn_Wq, L0_attn_Wk, L0_attn_Wv,
           L0_attn_Wo, L0_attn_g_weight, L0_ffn_W1, L0_ffn_b1, L0_ffn_W2,
           L0_ffn_b2):
    x = np.ascontiguousarray(np.asarray(L0_attn_hidden_states, dtype=np.float32))
    wqT = np.ascontiguousarray(np.asarray(L0_attn_Wq, dtype=np.float32).T)
    wkT = np.ascontiguousarray(np.asarray(L0_attn_Wk, dtype=np.float32).T)
    wvT = np.ascontiguousarray(np.asarray(L0_attn_Wv, dtype=np.float32).T)
    woT = np.ascontiguousarray(np.asarray(L0_attn_Wo, dtype=np.float32).T)
    g128 = np.ascontiguousarray(np.tile(np.asarray(L0_attn_g_weight,
                                                   dtype=np.float32),
                                        2).reshape(128, 1))
    w1T = np.ascontiguousarray(np.asarray(L0_ffn_W1, dtype=np.float32).T)
    b1r = np.ascontiguousarray(np.asarray(L0_ffn_b1,
                                          dtype=np.float32).reshape(32, 128).T)
    w2T = np.ascontiguousarray(np.asarray(L0_ffn_W2, dtype=np.float32).T)
    b2r = np.ascontiguousarray(np.asarray(L0_ffn_b2,
                                          dtype=np.float32).reshape(1, D))

    nc = build_program()
    shared = {"wqT": wqT, "wkT": wkT, "wvT": wvT, "woT": woT, "g128": g128,
              "w1T": w1T, "b1r": b1r, "w2T": w2T, "b2r": b2r}
    in_maps = [dict(shared, x=np.ascontiguousarray(x[c])) for c in range(8)]
    res = run_bass_kernel_spmd(nc, in_maps, core_ids=list(range(8)))
    out = np.stack([np.asarray(r["out"], dtype=np.float32)
                    for r in res.results], axis=0)
    return out


# version 21
# speedup vs baseline: 1.0846x; 1.0091x over previous
import contextlib
import math
import numpy as np

import concourse.bass as bass
import concourse.mybir as mybir
from concourse.bass_utils import run_bass_kernel_spmd

F32 = mybir.dt.float32
BF16 = mybir.dt.bfloat16
AF = mybir.ActivationFunctionType
ALU = mybir.AluOpType
AX = mybir.AxisListType.X

T = 2048
D = 1024
H = 16
E = 64
FF = 4096
NT = 16
KD = 8

SC = 1.0 / math.sqrt(32.0)
BI = 4.0 / math.sqrt(32.0)
KB = 1024


def strip_chunks(m):
    t0 = 128 * m
    first_end = 512 * (m // 4) + 512
    out = [(t0, first_end - t0)]
    b = first_end
    while b < T:
        out.append((b, 512))
        b += 512
    return out


def build_program():
    nc = bass.Bass("TRN2", target_bir_lowering=False)

    x_d = nc.dram_tensor("x", [T, D], F32, kind="ExternalInput").ap()
    wqT_d = nc.dram_tensor("wqT", [D, 256], F32, kind="ExternalInput").ap()
    wkT_d = nc.dram_tensor("wkT", [D, 256], F32, kind="ExternalInput").ap()
    wvT_d = nc.dram_tensor("wvT", [D, D], F32, kind="ExternalInput").ap()
    woT_d = nc.dram_tensor("woT", [D, D], F32, kind="ExternalInput").ap()
    g_d = nc.dram_tensor("g128", [128, 1], F32, kind="ExternalInput").ap()
    w1T_d = nc.dram_tensor("w1T", [D, FF], F32, kind="ExternalInput").ap()
    b1_d = nc.dram_tensor("b1r", [128, 32], F32, kind="ExternalInput").ap()
    w2T_d = nc.dram_tensor("w2T", [FF, D], F32, kind="ExternalInput").ap()
    b2_d = nc.dram_tensor("b2r", [1, D], F32, kind="ExternalInput").ap()
    out_d = nc.dram_tensor("out", [T, D], F32, kind="ExternalOutput").ap()
    hg_scr = nc.dram_tensor("hg_scr", [32, 128, T], BF16, kind="Internal").ap()

    SBUF_BASE = 16512

    def at(name, shape, dt, off_kb):
        return nc.alloc_sbuf_tensor_at(name, shape, dt,
                                       offset=SBUF_BASE + int(off_kb * KB))

    xnT = at("xnT", [128, NT, KD, 128], BF16, 0)
    w12_sb = at("w12_sb", [128, 32768], BF16, 32)
    jw1a = at("jw1a", [128, 2, T], BF16, 144)
    jw1b = at("jw1b", [128, 2, T], BF16, 152)
    jw2a = at("jw2a", [128, 2, D], BF16, 160)
    jw2b = at("jw2b", [128, 2, D], BF16, 164)
    w2t_sb = at("w2t_sb", [128, 32, D], BF16, 96)
    p1st = at("p1st", [128, 2, T], F32, 32)
    x_roll = at("x_roll", [128, 6, D], F32, 64)
    ajunk = at("ajunk", [128, T], BF16, 88)
    oneh = at("oneh", [128, 128], BF16, 200.25)
    tril05 = at("tril05", [128, 128], BF16, 200.5)
    wv_bf = at("wv_bf", [128, KD, D], BF16, 48)
    yT = at("yT", [128, KD, T], BF16, 64)
    xn_tm = at("xn_tm", [128, NT, D], BF16, 96)
    y_tm = at("y_tm", [128, 4, NT, 128], BF16, 96)
    wo_bf = at("wo_bf", [128, KD, D], BF16, 96)
    b2_bf = at("b2_bf", [1, D], BF16, 201)
    hgst = at("hgst", [128, 2, 512], BF16, 201)
    v_tm = at("v_tm", [128, NT, D], BF16, 128)
    wst = at("wst", [128, 2, T], F32, 176)
    hg_ts = at("hg_ts", [128, 2, 32, 256], BF16, 32)
    pv = at("pv", [128, 2, 512], BF16, 188)
    wq_bf = at("wq_bf", [128, KD, 256], BF16, 168)
    qT = at("qT", [128, 2, T], BF16, 160)
    junk = at("junk", [128, 2, D], BF16, 160)
    ysq = at("ysq", [128, D], BF16, 190)
    kT = at("kT", [128, 2, T], BF16, 168)
    x_re = at("x_re", [128, 2, D], F32, 168)
    u_sb = at("u_sb", [128, 2, D], F32, 176)
    zlhs = at("zlhs", [128, 128], BF16, 200)
    A_buf = at("A_buf", [128, 4, T], BF16, 112)
    wk_bf = at("wk_bf", [128, KD, 256], BF16, 188)
    qh_ext = at("qh_ext", [128, 2, T], BF16, 192)
    out_sb = at("out_sb", [128, 2, D], F32, 192)
    consts = at("consts", [128, 384], F32, 204)
    tril_bf = at("tril_bf", [128, 128], BF16, 205.5)
    ones1_bf = at("ones1_bf", [1, 128], BF16, 205.75)
    zrow_bf = at("zrow_bf", [1, 128], BF16, 207)
    rinv = at("rinv", [128, NT, H], F32, 206)


    C_S1, C_SS, C_A, C_SD, C_RC = 0, 16, 32, 48, 64
    C_W1P, C_W2P = 80, 96
    C_W1S, C_W2S, C_ONE, C_G = 112, 113, 114, 115
    C_T1, C_T1N, C_T2, C_T2N = 116, 117, 118, 119
    C_B1 = 128
    C_TMP = 160
    C_TMPB = 192
    C_THA, C_THB = 176, 177
    C_CBI, C_CSC, C_CS20, C_CI64, C_CEPS = 184, 185, 186, 187, 188
    C_CZ, C_CO = 189, 190
    C_TOT = 180
    C_OR = 256

    def cc(col, n=1):
        return consts[:, col:col + n]

    nc.const_aps.register(F32, 0.0, cc(C_CZ))
    nc.const_aps.register(F32, 1.0, cc(C_CO))

    psY = nc.alloc_psum_tensor("psY", [128, T], F32)
    psS = [nc.alloc_psum_tensor(f"psS{i}", [128, 512], F32) for i in range(4)]

    ctx = contextlib.ExitStack()
    def S(name):
        return ctx.enter_context(nc.semaphore(name))

    class DS:
        def __init__(self, name, n=2):
            self.n = n
            self.s = [S(name + chr(97 + k)) for k in range(n)]

        def inc(self, dma, i):
            dma.then_inc(self.s[i % self.n], 16)

        def wait_done(self, e, i):
            e.wait_ge(self.s[i % self.n], 16 * (i // self.n + 1))

        def wait_all(self, e, cnt):
            for k in range(self.n):
                if (cnt - k + self.n - 1) // self.n > 0:
                    e.wait_ge(self.s[k], 16 * ((cnt - k + self.n - 1) // self.n))
    dX = DS("dX", 6); sW = S("sW"); sWv = S("sWv"); sWb1 = S("sWb1"); sG0 = S("sG0")
    sLNu = S("sLNu"); sLNa2 = S("sLNa2"); sLNsq = S("sLNsq"); sLNsd = S("sLNsd")
    sXN = S("sXN"); dXNT = DS("dXNT")
    sPQ = S("sPQ"); sPE = S("sPE"); sVM = S("sVM")
    sEXT = S("sEXT"); sSC = S("sSC"); sPOLY = S("sPOLY"); sPOLYV = S("sPOLYV"); sDM = S("sDM")
    sAV = S("sAV"); sYB = S("sYB"); sYE = S("sYE"); sYT = S("sYT")
    sQZ = S("sQZ")
    dP1L = DS("dP1L"); sP1R = S("sP1R"); sTHR = S("sTHR"); sZL = S("sZL")
    sRMa = S("sRMa"); sRMb = S("sRMb"); sRMc = S("sRMc"); sRMcS = S("sRMcS")
    sLNr = S("sLNr"); sLNrc = S("sLNrc"); sRMr = S("sRMr")
    sFV = S("sFV")
    sYTT = S("sYTT"); dYTT = DS("dYTT"); sWOG = S("sWOG"); sWOGs = S("sWOGs")
    sWO = S("sWO"); sWOE = S("sWOE"); dOUT1 = DS("dOUT1"); dXRE = DS("dXRE")
    dW1L = DS("dW1L"); sW1T = S("sW1T")
    dW2L = DS("dW2L"); sW2T = S("sW2T")
    sH = S("sH"); sHGs = S("sHGs"); dHGW = DS("dHGW"); dHGR = DS("dHGR")
    sO2 = S("sO2"); sO2E = S("sO2E"); dOUT2 = DS("dOUT2")

    poly_idx = {}
    dm_of = {}
    strip_of = {}
    n = 0; nd = 0; ns = 0
    for p in range(8):
        for m in range(NT):
            for hi in range(2):
                strip_of[(p, m, hi)] = ns; ns += 1
                for ci in range(len(strip_chunks(m))):
                    poly_idx[(p, m, hi, ci)] = n; n += 1
                dm_of[(p, m, hi)] = nd; nd += 1
    N_CH = n
    last_sc_of_pair = {p: poly_idx[(p, NT - 1, 1, 0)] + 1 for p in range(8)}
    owner = {}
    cntS = {}
    cntV = {}
    nS = 0
    nV = 0
    for idx in range(N_CH):
        if idx % 3 == 2:
            owner[idx] = "V"
            nV += 1
        else:
            owner[idx] = "S"
            nS += 1
        cntS[idx] = nS
        cntV[idx] = nV
    N_CH_S, N_CH_V = nS, nV

    def xt(t):
        return x_roll[:, t % 6, :]

    def blk(ap2d, nb):
        return ap2d.rearrange("p (b i) -> p b i", b=nb, i=128)

    def sync_stream(e):
        e.dma_start(out=cc(C_G), in_=g_d).then_inc(sG0, 16)
        e.dma_start(out=consts[:, C_B1:C_B1 + 32], in_=b1_d).then_inc(sWb1, 16)
        for t in range(NT):
            if t >= 6:
                e.wait_ge(sLNu, t - 5)
            dX.inc(e.dma_start(out=xt(t), in_=x_d[t * 128:(t + 1) * 128, :]), t)
        for t in range(NT):
            e.wait_ge(sXN, t + 1)
            if t >= 2:
                dXNT.wait_done(e, t - 2)
            dXNT.inc(e.dma_start(out=xnT[:, t, :, :],
                                 in_=blk(xn_tm[:, t, :], KD),
                                 transpose=True), t)
        e.wait_ge(sPE, 8)
        for h in (0, 1):
            r0 = 16 * (h % 8)
            e.wait_ge(sQZ, 1)
            e.dma_start(out=qh_ext[r0:r0 + 16, h % 2, :],
                        in_=qT[r0:r0 + 16, h // 8, :]).then_inc(sEXT, 16)
        for p in range(8):
            for h in (2 * p + 2, 2 * p + 3):
                if h < 16:
                    e.wait_ge(sSC, last_sc_of_pair[p])
                    e.wait_ge(sQZ, 2 * p - 1 if p >= 2 else 1)
                    r0 = 16 * (h % 8)
                    e.dma_start(out=qh_ext[r0:r0 + 16, h % 2, :],
                                in_=qT[r0:r0 + 16, h // 8, :]).then_inc(sEXT, 16)
            if p >= 3:
                pp = p - 3
                e.wait_ge(sRMc, NT * (pp + 1))
                e.wait_ge(sRMcS, NT * (pp + 1))
                if pp >= 2:
                    dYTT.wait_done(e, pp - 2)
                dYTT.inc(e.dma_start(out=blk(yT[:, pp, :], NT),
                                     in_=y_tm[:, pp % 4, :, :],
                                     transpose=True), pp)
        for pp in (5, 6, 7):
            e.wait_ge(sRMc, NT * (pp + 1))
            e.wait_ge(sRMcS, NT * (pp + 1))
            dYTT.wait_done(e, pp - 2)
            dYTT.inc(e.dma_start(out=blk(yT[:, pp, :], NT),
                                 in_=y_tm[:, pp % 4, :, :],
                                 transpose=True), pp)
        for t in range(NT):
            e.wait_ge(sWOE, 2 * (t + 1))
            dOUT1.inc(e.dma_start(out=out_d[t * 128:(t + 1) * 128, :],
                                  in_=out_sb[:, t % 2, :]), t)
        for i in range(128):
            tb, j = i // 32, i % 32
            e.wait_ge(sHGs, i + 1)
            dHGW.inc(e.dma_start(out=hg_scr[j][:, tb * 512:(tb + 1) * 512],
                                 in_=hgst[:, i % 2, :]), i)
        e.wait_ge(sH, 128)
        for k in range(8):
            dHGW.wait_all(e, 32 * (k // 2 + 1))
            if k >= 1:
                e.wait_ge(sO2, 4 * k - 2)
            for part in range(4):
                j0 = part * 8
                src = bass.AP(hg_scr.tensor, (j0 * 128) * T + 2 * k * 128,
                              [[T, 128], [128 * T, 8], [1, 256]])
                d = e.dma_start(out=hg_ts[:, k % 2, j0:j0 + 8, :], in_=src)
                d.then_inc(dHGR.s[k % 2], 16)

    def gp_stream(e):
        e.dma_start(out=wq_bf[:], in_=wqT_d.rearrange("(c p) f -> p c f", p=128)
                    ).then_inc(sW, 16)
        e.dma_start(out=wk_bf[:], in_=wkT_d.rearrange("(c p) f -> p c f", p=128)
                    ).then_inc(sW, 16)
        e.iota(u_sb[:, 0, 0:128], [[1, 128]], base=0, channel_multiplier=-1,
               allow_small_or_imprecise_dtypes=True).then_inc(sG0, 1)
        e.dma_start(out=wv_bf[:], in_=wvT_d.rearrange("(c p) f -> p c f", p=128)
                    ).then_inc(sWv, 16)
        w1v = w1T_d.rearrange("(c p) f -> p c f", p=128)
        w2v = w2T_d.rearrange("(c p) f -> p c f", p=128)
        for i in range(32):
            if i >= 2:
                e.wait_ge(sP1R, i - 1)
            if i < 16:
                src = w1v[:, i // 2, (i % 2) * T:(i % 2 + 1) * T]
            else:
                src = w2v[:, 2 * (i - 16):2 * (i - 16) + 2, :]
            dst = p1st[:, i % 2, :]
            if i >= 16:
                dst = dst.rearrange("p (a b) -> p a b", a=2, b=1024)
            dP1L.inc(e.dma_start(out=dst, in_=src), i)
        e.wait_ge(sXN, NT)
        e.memset(zlhs[:], 0.0).then_inc(sZL, 1)
        dYTT.wait_all(e, KD)
        e.dma_start(out=wo_bf[:], in_=woT_d.rearrange("(c p) f -> p c f", p=128)
                    ).then_inc(sWOG, 16)
        e.dma_start(out=b2_bf[:], in_=b2_d).then_inc(sWOG, 16)
        e.wait_ge(sYE, 128)
        for u in range(16):
            c, hf = u // 2, u % 2
            if u >= 2:
                e.wait_ge(sW1T, u - 1)
            dW1L.inc(e.dma_start(out=wst[:, u % 2, :],
                                 in_=w1T_d[c * 128:(c + 1) * 128,
                                           hf * T:(hf + 1) * T]), u)
        e.wait_ge(sW1T, 16)
        for j in range(32):
            if j >= 2:
                e.wait_ge(sW2T, j - 1)
            dW2L.inc(e.dma_start(out=wst[:, j % 2, 0:D],
                                 in_=w2T_d[j * 128:(j + 1) * 128, :]), j)
        dOUT1.wait_all(e, NT)
        for t in (0, 1):
            dXRE.inc(e.dma_start(out=x_re[:, t, :],
                                 in_=x_d[t * 128:(t + 1) * 128, :]), t)
        for ts in range(NT):
            if ts + 2 < NT:
                tt = ts + 2
                e.wait_ge(sO2E, 2 * (tt - 1))
                dXRE.inc(e.dma_start(out=x_re[:, tt % 2, :],
                                     in_=x_d[tt * 128:(tt + 1) * 128, :]), tt)
            e.wait_ge(sO2E, 2 * (ts + 1))
            dOUT2.inc(e.dma_start(out=out_d[ts * 128:(ts + 1) * 128, :],
                                  in_=out_sb[:, ts % 2, :],
                                  accum_op=ALU.add), ts)
        dOUT1.wait_all(e, NT)
        dOUT2.wait_all(e, NT)

    def vec_stream(e):
        e.wait_ge(sG0, 17)
        fv = [0]

        def pub(ins):
            fv[0] += 1
            ins.then_inc(sFV, 1)

        def fence():
            e.wait_ge(sFV, fv[0])

        pub(e.tensor_single_scalar(tril_bf[:], u_sb[:, 0, 0:128], 0.0,
                                   ALU.is_ge))
        e.memset(qh_ext[:], 0.0).then_inc(sQZ, 1)
        pub(e.memset(ones1_bf[:], 1.0))
        pub(e.memset(zrow_bf[:], 0.0))
        pub(e.memset(oneh[:], 0.5))
        pub(e.memset(cc(C_ONE), 1.0))
        pub(e.memset(consts[0:1, C_OR:C_OR + 128], 1.0))
        pub(e.memset(cc(C_CBI), BI))
        pub(e.memset(cc(C_CSC), SC))
        pub(e.memset(cc(C_CS20), float(2.0 ** 20)))
        pub(e.memset(cc(C_CI64), 1.0 / E))
        pub(e.memset(cc(C_CEPS), 1e-5))
        pub(e.memset(cc(C_CZ), 0.0))
        pub(e.memset(cc(C_CO), 1.0))
        fence()
        e.tensor_scalar(tril05[:], tril_bf[:], 0.5, None,
                        ALU.mult).then_inc(sVM, 1)
        fence()
        for tp in range(0, NT, 2):
            for t in (tp, tp + 1):
                dX.wait_done(e, t)
                e.tensor_reduce(cc(C_S1 + t), xt(t), AX,
                                ALU.add).then_inc(sLNr, 1)
            for t in (tp, tp + 1):
                if t >= 2:
                    e.wait_ge(sLNsq, t - 1)
                e.wait_ge(sLNr, t + 1)
                e.tensor_scalar(u_sb[:, t % 2, :], xt(t), 1024.0,
                                cc(C_S1 + t), ALU.mult,
                                ALU.subtract).then_inc(sLNu, 1)
            for t in (tp, tp + 1):
                e.wait_ge(sLNsq, t + 1)
                e.tensor_scalar(cc(C_A + t), cc(C_SS + t), 2.0 ** -30, 1e-5,
                                ALU.mult, ALU.add).then_inc(sLNa2, 1)
            for t in (tp, tp + 1):
                e.wait_ge(sLNsd, t + 1)
                e.reciprocal(cc(C_RC + t), cc(C_SD + t)).then_inc(sLNrc, 1)
            for t in (tp, tp + 1):
                e.wait_ge(sLNrc, t + 1)
                e.tensor_scalar(xn_tm[:, t, :], u_sb[:, t % 2, :],
                                cc(C_RC + t), None, ALU.mult).then_inc(sXN, 1)
        pvk = [0]

        def rms_vec(p):
            e.wait_ge(sYE, 16 * (p + 1))
            for k in range(2):
                if p + k >= 1:
                    e.wait_ge(sRMa, 2 * p + k)
                yv = y_tm[:, p % 4, 8 * k:8 * k + 8, :]
                pub(e.tensor_tensor(blk(ysq[:], 8), yv, yv, ALU.mult))
                fence()
                rv = rinv[:, 8 * k:8 * k + 8, 2 * p:2 * p + 2]
                e.tensor_reduce(rv,
                                ysq[:].rearrange("p (t b w) -> p t b w",
                                                 t=8, b=2, w=E),
                                AX, ALU.add).then_inc(sRMa, 1)

        def rms_vec2(p):
            e.wait_ge(sRMb, p + 1)
            e.reciprocal(rinv[:, :, 2 * p:2 * p + 2],
                         rinv[:, :, 2 * p:2 * p + 2]).then_inc(sRMr, 1)
            e.wait_ge(sRMr, p + 1)
            h = 2 * p
            for t in range(NT):
                sl = y_tm[:, p % 4, t, 0:E]
                e.tensor_scalar(sl, sl, rinv[:, t, h:h + 1], None,
                                ALU.mult).then_inc(sRMc, 1)

        def evac_t(p, t):
            if t == 0 and p >= 4:
                dYTT.wait_done(e, p - 4)
            e.wait_ge(sYB, 4 * p + t // 4 + 1)
            e.tensor_copy(y_tm[:, p % 4, t, :],
                          psY[:, 128 * t:128 * (t + 1)]).then_inc(sYE, 1)

        for p in range(8):
            if p == 1:
                e.wait_ge(sP1R, 32)
                pub(e.tensor_reduce(cc(C_W1S), consts[:, C_W1P:C_W1P + 16],
                                    AX, ALU.add))
                fence()
                e.tensor_reduce(cc(C_W2S), consts[:, C_W2P:C_W2P + 16], AX,
                                ALU.add).then_inc(sTHR, 1)
            if p == 2:
                e.wait_ge(sTHR, 4)
                e.tensor_scalar(cc(C_THA), psY[:, 1536:1537], 1.0 / (FF * D),
                                1e-8, ALU.mult, ALU.max).then_inc(sTHR, 1)
                e.tensor_scalar(cc(C_THB), psY[:, 1537:1538], 1.0 / (FF * D),
                                1e-8, ALU.mult, ALU.max).then_inc(sTHR, 1)
                e.wait_ge(sTHR, 6)
                e.tensor_scalar(cc(C_T1), cc(C_THA), 0.5, None,
                                ALU.mult).then_inc(sTHR, 1)
                e.tensor_scalar(cc(C_T1N), cc(C_THA), -0.5, None,
                                ALU.mult).then_inc(sTHR, 1)
                e.tensor_scalar(cc(C_T2), cc(C_THB), 0.5, None,
                                ALU.mult).then_inc(sTHR, 1)
                e.tensor_scalar(cc(C_T2N), cc(C_THB), -0.5, None,
                                ALU.mult).then_inc(sTHR, 1)
            if p >= 1:
                b0 = 32 * ((p - 1) % 4)
                for hs in (0, 1):
                    e.wait_ge(sSC, last_sc_of_pair[p - 1])
                    e.memset(qh_ext[b0:b0 + 32, hs, :], 0.0).then_inc(sQZ, 1)
                rms_vec(p - 1)
            if p >= 2:
                rms_vec2(p - 2)
            for m in range(NT):
                chs = strip_chunks(m)
                for hi in range(2):
                    gs = strip_of[(p, m, hi)]
                    gated = False
                    for ci, (t0, w) in enumerate(chs):
                        idx = poly_idx[(p, m, hi, ci)]
                        if owner[idx] != "V":
                            continue
                        if gs >= 4 and not gated:
                            e.wait_ge(sAV, gs - 3)
                            gated = True
                        e.wait_ge(sSC, idx + 1)
                        slot = pvk[0] % 2
                        pvk[0] += 1
                        pub(e.tensor_scalar(pv[:, slot, 0:w],
                                            psS[idx % 4][:, 0:w],
                                            SC, BI, ALU.mult, ALU.add))
                        fence()
                        e.tensor_tensor(A_buf[:, gs % 4, t0:t0 + w],
                                        pv[:, slot, 0:w], pv[:, slot, 0:w],
                                        ALU.mult).then_inc(sPOLYV, 1)
                for hi in range(2):
                    gs = strip_of[(p, m, hi)]
                    idx = poly_idx[(p, m, hi, 0)]
                    if owner[idx] == "S":
                        e.wait_ge(sPOLY, cntS[idx])
                    else:
                        e.wait_ge(sPOLYV, cntV[idx])
                    sl = A_buf[:, gs % 4, 128 * m:128 * m + 128]
                    e.tensor_tensor(sl, sl, tril_bf[:],
                                    ALU.mult).then_inc(sDM, 1)
                if m >= 4 and m % 4 == 0:
                    for tt in range(m - 4, m):
                        evac_t(p, tt)
            for tt in range(12, 16):
                evac_t(p, tt)
        rms_vec(7)
        rms_vec2(6)
        rms_vec2(7)
        e.wait_ge(sWOG, 32)
        for c in range(KD):
            e.tensor_scalar(wo_bf[:, c, :], wo_bf[:, c, :], cc(C_G),
                            None, ALU.mult).then_inc(sWOGs, 1)
        def w1_tern_unit(u):
            c, hf = u // 2, u % 2
            e.wait_ge(sTHR, 10)
            dW1L.wait_done(e, u)
            if c >= 4:
                e.wait_ge(sWO, 2 * NT)
            srcv = wst[:, u % 2, :]
            pub(e.tensor_single_scalar(jw1a[:, u % 2, :], srcv, cc(C_T1),
                                       ALU.is_gt))
            pub(e.tensor_single_scalar(jw1b[:, u % 2, :], srcv, cc(C_T1N),
                                       ALU.is_lt))
            fence()
            e.tensor_tensor(w12_sb[:, c * FF + hf * T:c * FF + (hf + 1) * T],
                            jw1a[:, u % 2, :], jw1b[:, u % 2, :],
                            ALU.subtract).then_inc(sW1T, 1)

        def wo_bank_v(i):
            if i % 4 < 2:
                return psS[2 + (i % 4)][:]
            b = i % 4 - 2
            return psY[:, 512 * b:512 * (b + 1)]

        for u in range(16):
            w1_tern_unit(u)
        def o2_bank_v(i):
            if i % 4 < 2:
                return psS[2 + (i % 4)][:]
            b = i % 4 - 2
            return psY[:, 512 * b:512 * (b + 1)]

        def o2_evac(ts):
            for dh in range(2):
                i = 2 * ts + dh
                ps = o2_bank_v(i)
                e.wait_ge(sO2, i + 1)
                dXRE.wait_done(e, ts)
                if ts >= 2:
                    dOUT2.wait_done(e, ts - 2)
                e.tensor_tensor(out_sb[:, ts % 2, dh * 512:(dh + 1) * 512],
                                ps, x_re[:, ts % 2, dh * 512:(dh + 1) * 512],
                                ALU.add).then_inc(sO2E, 1)

        for j in range(32):
            e.wait_ge(sTHR, 10)
            if j == 0:
                e.wait_ge(sWO, 2 * NT)
            dW2L.wait_done(e, j)
            src = wst[:, j % 2, 0:D]
            pub(e.tensor_single_scalar(jw2a[:, j % 2, :], src, cc(C_T2),
                                       ALU.is_gt))
            pub(e.tensor_single_scalar(jw2b[:, j % 2, :], src, cc(C_T2N),
                                       ALU.is_lt))
            fence()
            e.tensor_tensor(w2t_sb[:, j, :], jw2a[:, j % 2, :],
                            jw2b[:, j % 2, :],
                            ALU.subtract).then_inc(sW2T, 1)
        dOUT1.wait_all(e, NT)
        for ts in range(NT):
            o2_evac(ts)

    proj = ([("q", fc, ts) for fc in range(2) for ts in range(4)]
            + [("k", fc, ts) for fc in range(2) for ts in range(4)]
            + [("v", eh, tt) for eh in range(2) for tt in range(NT)])

    def sca_stream(e):
        ab_i = [0]

        def abs_unit():
            i = ab_i[0]
            if i >= 32:
                return
            ab_i[0] += 1
            if i >= 1:
                e.wait_ge(sP1R, i)
            dP1L.wait_done(e, i)
            col = (C_W1P + i) if i < 16 else (C_W2P + i - 16)
            e.activation(ajunk[:], p1st[:, i % 2, :], AF.Abs,
                         accum_out=cc(col)).then_inc(sP1R, 1)

        for tp in range(0, NT, 2):
            for t in (tp, tp + 1):
                e.wait_ge(sLNu, t + 1)
                e.activation(junk[:, t % 2, :], u_sb[:, t % 2, :], AF.Square,
                             accum_out=cc(C_SS + t)).then_inc(sLNsq, 1)
            abs_unit()
            abs_unit()
            for t in (tp, tp + 1):
                e.wait_ge(sLNa2, t + 1)
                e.activation(cc(C_SD + t), cc(C_A + t), AF.Sqrt,
                             scale=cc(C_CS20)).then_inc(sLNsd, 1)
            abs_unit()
            abs_unit()
        while ab_i[0] < 32:
            abs_unit()
        for i, (kind, a, b) in enumerate(proj):
            ps = psS[i % 4]
            e.wait_ge(sPQ, i + 1)
            if kind == "q":
                dst = qT[:, a, b * 512:(b + 1) * 512]
            elif kind == "k":
                dst = kT[:, a, b * 512:(b + 1) * 512]
            else:
                dst = v_tm[:, b, a * 512:(a + 1) * 512]
            e.copy(dst, ps[:]).then_inc(sPE, 1)
        def rms_sca(p):
            e.wait_ge(sRMa, 2 * (p + 1))
            e.activation(rinv[:, :, 2 * p:2 * p + 2],
                         rinv[:, :, 2 * p:2 * p + 2], AF.Sqrt,
                         scale=cc(C_CI64), bias=cc(C_CEPS)).then_inc(sRMb, 1)

        def rms_sca2(p):
            e.wait_ge(sRMr, p + 1)
            h = 2 * p + 1
            for t in range(NT):
                sl = y_tm[:, p % 4, t, E:2 * E]
                e.activation(sl, sl, AF.Copy,
                             scale=rinv[:, t, h:h + 1]).then_inc(sRMcS, 1)

        for p in range(8):
            if p == 1:
                e.wait_ge(sTHR, 2)
                e.copy(consts[0:1, C_TOT:C_TOT + 2],
                       psY[0:1, 1536:1538]).then_inc(sTHR, 1)
            if p >= 2:
                rms_sca(p - 2)
            if p >= 3:
                rms_sca2(p - 3)
            for m in range(NT):
                chs = strip_chunks(m)
                for hi in range(2):
                    gs = strip_of[(p, m, hi)]
                    gated = False
                    for ci, (t0, w) in enumerate(chs):
                        idx = poly_idx[(p, m, hi, ci)]
                        if owner[idx] != "S":
                            continue
                        if gs >= 4 and not gated:
                            e.wait_ge(sAV, gs - 3)
                            gated = True
                        e.wait_ge(sSC, idx + 1)
                        e.activation(A_buf[:, gs % 4, t0:t0 + w],
                                     psS[idx % 4][:, 0:w], AF.Square,
                                     bias=cc(C_CBI),
                                     scale=cc(C_CSC)).then_inc(sPOLY, 1)
        for p in (6, 7):
            rms_sca(p)
        for p in (5, 6, 7):
            rms_sca2(p)
        def wo_bank_s(i):
            if i % 4 < 2:
                return psS[2 + (i % 4)][:]
            b = i % 4 - 2
            return psY[:, 512 * b:512 * (b + 1)]

        for t in range(NT):
            for dh in range(2):
                i = 2 * t + dh
                ps = wo_bank_s(i)
                e.wait_ge(sWO, i + 1)
                if t >= 2:
                    dOUT1.wait_done(e, t - 2)
                e.copy(out_sb[:, t % 2, dh * 512:(dh + 1) * 512],
                       ps).then_inc(sWOE, 1)
        e.wait_ge(sWb1, 16)
        for i in range(128):
            j = i % 32
            ps = psS[i % 2]
            e.wait_ge(sH, i + 1)
            if i >= 2:
                dHGW.wait_done(e, i - 2)
            e.activation(hgst[:, i % 2, :], ps[:], AF.Gelu,
                         bias=consts[:, C_B1 + j:C_B1 + j + 1]).then_inc(sHGs, 1)
        def o2_bank_s(i):
            if i % 4 < 2:
                return psS[2 + (i % 4)][:]
            b = i % 4 - 2
            return psY[:, 512 * b:512 * (b + 1)]

        dOUT1.wait_all(e, NT)


    def ten_stream(e):
        e.wait_ge(sW, 32)
        for i, (kind, a, b) in enumerate(proj):
            ps = psS[i % 4]
            if i == 16:
                e.wait_ge(sWv, 16)
            if i >= 4:
                e.wait_ge(sPE, i - 3)
            if kind == "v":
                dXNT.wait_done(e, b)
            else:
                dXNT.wait_all(e, 4 * (b + 1))
            for c in range(KD):
                if kind == "q":
                    lhsT = wq_bf[:, c, a * 128:(a + 1) * 128]
                    rhs = xnT[:, 4 * b:4 * b + 4, c, :]
                elif kind == "k":
                    lhsT = wk_bf[:, c, a * 128:(a + 1) * 128]
                    rhs = xnT[:, 4 * b:4 * b + 4, c, :]
                else:
                    lhsT = xnT[:, b, c, :]
                    rhs = wv_bf[:, c, a * 512:(a + 1) * 512]
                mm = e.matmul(ps[:], lhsT, rhs, start=(c == 0),
                              stop=(c == KD - 1))
            mm.then_inc(sPQ, 1)
        e.wait_ge(sPE, 48)
        e.wait_ge(sVM, 1)

        def emit_av(e, p, m):
            chs = strip_chunks(m)

            def tile_ci(t):
                col = 128 * t
                for ci, (t0, w) in enumerate(chs):
                    if t0 <= col < t0 + w:
                        return ci
                raise AssertionError

            for hi, h in ((0, 2 * p), (1, 2 * p + 1)):
                gs = strip_of[(p, m, hi)]
                prev_ci = -1
                for t in list(range(m + 1, NT)) + [m]:
                    if t == m:
                        e.wait_ge(sDM, dm_of[(p, m, hi)] + 1)
                    else:
                        ci = tile_ci(t)
                        if ci != prev_ci:
                            idx = poly_idx[(p, m, hi, ci)]
                            if owner[idx] == "S":
                                e.wait_ge(sPOLY, cntS[idx])
                            else:
                                e.wait_ge(sPOLYV, cntV[idx])
                            prev_ci = ci
                    e.matmul(psY[:, 128 * t + 64 * hi:128 * t + 64 * hi + 64],
                             A_buf[:, gs % 4, 128 * t:128 * (t + 1)],
                             v_tm[:, m, h * E:(h + 1) * E],
                             start=False, stop=False)
                e.sem_inc(sAV, 1)

        def close_b(e, b):
            e.matmul(psY[:, 512 * b:512 * (b + 1)], zrow_bf[:],
                     v_tm[0:1, 0, 0:512], start=False,
                     stop=True).then_inc(sYB, 1)

        for p in range(8):
            for b in range(4):
                if p > 0:
                    e.wait_ge(sYE, 16 * (p - 1) + 4 * (b + 1))
                if b == 3 and p == 1:
                    e.wait_ge(sTHR, 3)
                if b == 3 and p == 2:
                    e.wait_ge(sTHR, 6)
                e.matmul(psY[:, 512 * b:512 * (b + 1)], zrow_bf[:],
                         v_tm[0:1, 0, 0:512], start=True, stop=False)
            for mm_ in range(NT):
                for t in range(mm_, NT):
                    e.matmul(psY[:, 128 * t:128 * (t + 1)],
                             tril05[:] if t == mm_ else oneh[:],
                             v_tm[:, mm_, 128 * p:128 * (p + 1)],
                             start=False, stop=False)
            for m in range(NT):
                chs = strip_chunks(m)
                if m == 0 and p >= 1:
                    e.wait_ge(sQZ, 1 + 2 * p)
                for hi, h in ((0, 2 * p), (1, 2 * p + 1)):
                    if m == 0 and hi == 0:
                        e.wait_ge(sEXT, 32 * (p + 1))
                    for ci, (t0, w) in enumerate(chs):
                        idx = poly_idx[(p, m, hi, ci)]
                        if idx >= 4:
                            oidx = idx - 4
                            if owner[oidx] == "S":
                                e.wait_ge(sPOLY, cntS[oidx])
                            else:
                                e.wait_ge(sPOLYV, cntV[oidx])
                        e.matmul(psS[idx % 4][:, 0:w],
                                 kT[:, h // 8, m * 128:(m + 1) * 128],
                                 qh_ext[:, hi, t0:t0 + w],
                                 start=True, stop=True).then_inc(sSC, 1)
                if m > 0:
                    emit_av(e, p, m - 1)
                    if (m - 1) % 4 == 3:
                        close_b(e, (m - 1) // 4)
            emit_av(e, p, NT - 1)
            close_b(e, 3)
            if p == 0:
                e.wait_ge(sYE, 16)
                e.wait_ge(sTHR, 1)
                e.matmul(psY[0:1, 1536:1537], cc(C_W1S), cc(C_ONE),
                         start=True, stop=True)
                e.matmul(psY[0:1, 1537:1538], cc(C_W2S), cc(C_ONE),
                         start=True, stop=True).then_inc(sTHR, 1)
            if p == 1:
                e.wait_ge(sYE, 32)
                e.wait_ge(sTHR, 3)
                e.matmul(psY[:, 1536:1537], consts[0:1, C_OR:C_OR + 128],
                         consts[0:1, C_TOT:C_TOT + 1], start=True, stop=True)
                e.matmul(psY[:, 1537:1538], consts[0:1, C_OR:C_OR + 128],
                         consts[0:1, C_TOT + 1:C_TOT + 2], start=True,
                         stop=True).then_inc(sTHR, 1)
        dYTT.wait_all(e, KD)
        e.wait_ge(sWOG, 32)
        e.wait_ge(sWOGs, 8)
        def wo_bank(i):
            if i % 4 < 2:
                return psS[2 + (i % 4)][:]
            b = i % 4 - 2
            return psY[:, 512 * b:512 * (b + 1)]

        for t in range(NT):
            for dh in range(2):
                i = 2 * t + dh
                ps = wo_bank(i)
                if i >= 4:
                    e.wait_ge(sWOE, i - 3)
                for c in range(KD):
                    e.matmul(ps, yT[:, c, t * 128:(t + 1) * 128],
                             wo_bf[:, c, dh * 512:(dh + 1) * 512],
                             start=(c == 0), stop=False)
                e.matmul(ps, ones1_bf[:],
                         b2_bf[0:1, dh * 512:(dh + 1) * 512],
                         start=False, stop=True).then_inc(sWO, 1)
        e.wait_ge(sW1T, 16)
        dXNT.wait_all(e, NT)
        for i in range(128):
            tb, j = i // 32, i % 32
            ps = psS[i % 2]
            if i >= 2:
                e.wait_ge(sHGs, i - 1)
            for c in range(KD):
                mm = e.matmul(ps[:],
                              w12_sb[:, c * FF + j * 128:c * FF + j * 128 + 128],
                              xnT[:, 4 * tb:4 * tb + 4, c, :],
                              start=(c == 0), stop=(c == KD - 1))
            mm.then_inc(sH, 1)
        def o2_bank(i):
            if i % 4 < 2:
                return psS[2 + (i % 4)][:]
            b = i % 4 - 2
            return psY[:, 512 * b:512 * (b + 1)]

        for ts in range(NT):
            k = ts // 2
            e.wait_ge(dHGR.s[k % 2], 64 * (k // 2 + 1))
            for dh in range(2):
                i = 2 * ts + dh
                ps = o2_bank(i)
                if i >= 4:
                    e.wait_ge(sO2E, i - 3)
                for j in range(32):
                    if ts == 0 and dh == 0:
                        e.wait_ge(sW2T, j + 1)
                    mm = e.matmul(ps,
                                  hg_ts[:, k % 2, j,
                                        (ts % 2) * 128:(ts % 2) * 128 + 128],
                                  w2t_sb[:, j, dh * 512:(dh + 1) * 512],
                                  start=(j == 0), stop=(j == 31))
                mm.then_inc(sO2, 1)

    with ctx:
        with nc.Block() as block:
            block.sync(sync_stream)
            block.tensor(ten_stream)
            block.vector(vec_stream)
            block.scalar(sca_stream)
            block.gpsimd(gp_stream)
    return nc


def kernel(L0_attn_hidden_states, L0_attn_Wq, L0_attn_Wk, L0_attn_Wv,
           L0_attn_Wo, L0_attn_g_weight, L0_ffn_W1, L0_ffn_b1, L0_ffn_W2,
           L0_ffn_b2):
    x = np.ascontiguousarray(np.asarray(L0_attn_hidden_states, dtype=np.float32))
    wqT = np.ascontiguousarray(np.asarray(L0_attn_Wq, dtype=np.float32).T)
    wkT = np.ascontiguousarray(np.asarray(L0_attn_Wk, dtype=np.float32).T)
    wvT = np.ascontiguousarray(np.asarray(L0_attn_Wv, dtype=np.float32).T)
    woT = np.ascontiguousarray(np.asarray(L0_attn_Wo, dtype=np.float32).T)
    g128 = np.ascontiguousarray(np.tile(np.asarray(L0_attn_g_weight,
                                                   dtype=np.float32),
                                        2).reshape(128, 1))
    w1T = np.ascontiguousarray(np.asarray(L0_ffn_W1, dtype=np.float32).T)
    b1r = np.ascontiguousarray(np.asarray(L0_ffn_b1,
                                          dtype=np.float32).reshape(32, 128).T)
    w2T = np.ascontiguousarray(np.asarray(L0_ffn_W2, dtype=np.float32).T)
    b2r = np.ascontiguousarray(np.asarray(L0_ffn_b2,
                                          dtype=np.float32).reshape(1, D))

    nc = build_program()
    shared = {"wqT": wqT, "wkT": wkT, "wvT": wvT, "woT": woT, "g128": g128,
              "w1T": w1T, "b1r": b1r, "w2T": w2T, "b2r": b2r}
    in_maps = [dict(shared, x=np.ascontiguousarray(x[c])) for c in range(8)]
    res = run_bass_kernel_spmd(nc, in_maps, core_ids=list(range(8)))
    out = np.stack([np.asarray(r["out"], dtype=np.float32)
                    for r in res.results], axis=0)
    return out
